# version 21
# baseline (speedup 1.0000x reference)
import math
from contextlib import ExitStack

import numpy as np
import ml_dtypes
import concourse.bass as bass
import concourse.mybir as mybir
from concourse.bass_utils import run_bass_kernel_spmd

F32 = mybir.dt.float32
BF16 = mybir.dt.bfloat16
I32 = mybir.dt.int32
AF = mybir.ActivationFunctionType
ALU = mybir.AluOpType
AX = mybir.AxisListType
BF = ml_dtypes.bfloat16

T = 2048
D = 4096
SLAB = 1024
NEG = -1.0e30

ENGINES = ("sync", "tensor", "vector", "scalar", "gpsimd")


class Buf:
    __slots__ = ("name", "w_tok", "readers")

    def __init__(self, name="b", readers=None):
        self.name = name
        self.w_tok = None
        self.readers = list(readers) if readers else []


class Prog:
    def __init__(self, nc):
        self.nc = nc
        self.counts = {e: 0 for e in ENGINES}
        self.ops = {e: [] for e in ENGINES}
        self.sems = {}
        self.scope = "init"

    def dma_token(self, stream):
        return ("D" + stream, self.counts.get("D" + stream, 0))

    def op(self, eng, fn, reads=(), writes=(), deps=(), dma=None):
        dep_toks = []
        for b in reads:
            if b.w_tok is not None:
                dep_toks.append(b.w_tok)
        for b in writes:
            if b.w_tok is not None:
                dep_toks.append(b.w_tok)
            dep_toks.extend(b.readers)
        dep_toks.extend(deps)
        merged = {}
        for (e2, v) in dep_toks:
            if e2 == eng and v > self.counts[eng]:
                raise RuntimeError("forward self dep")
            if e2 == "tensor" and eng == "tensor":
                continue
            if merged.get(e2, -1) < v:
                merged[e2] = v
        if dma is None:
            tok = (eng, self.counts[eng] + 1)
            self.counts[eng] = tok[1]
        else:
            key = "D" + dma
            tok = (key, self.counts.get(key, 0) + 16)
            self.counts[key] = tok[1]
        self.ops[eng].append((fn, merged, tok, self.scope))
        for b in reads:
            b.readers.append(tok)
        for b in writes:
            b.w_tok = tok
            b.readers = []
        return tok

    def finalize(self, block):
        for key in self.counts:
            self.sems[key] = self.nc.alloc_semaphore(name=f"sem_{key}")

        def make_body(ename):
            def body(eng):
                waited = {}
                cur = None
                sid = None
                for (fn, merged, tok, scope) in self.ops[ename]:
                    if scope != cur:
                        if cur is not None:
                            self.nc.leave_named_scope(cur, sid, False)
                        sid, _ = self.nc.enter_named_scope(scope, False)
                        cur = scope
                    for e2, v in sorted(merged.items()):
                        if waited.get(e2, 0) < v:
                            eng.wait_ge(self.sems[e2], v)
                            waited[e2] = v
                    key, val = tok
                    fn(eng).then_inc(self.sems[key],
                                     16 if key[0] == "D" else 1)
                if cur is not None:
                    self.nc.leave_named_scope(cur, sid, False)
                if ename == "sync":
                    for e2, cnt in self.counts.items():
                        if e2 != "sync" and cnt > waited.get(e2, 0):
                            eng.wait_ge(self.sems[e2], cnt)
            return body

        for e in ENGINES:
            getattr(block, e)(make_body(e))


class Region:

    def __init__(self, ctx, name, kb_per_partition):
        self.n = kb_per_partition * 512
        self.back = ctx.ex.enter_context(
            ctx.nc.sbuf_tensor(name, [128, self.n], BF16))
        self.off = 0
        self.live = []
        self.pending = []

    def carve(self, name, shape, dtype):
        assert shape[0] <= 128
        elems = int(np.prod(shape[1:]))
        width = elems * (2 if dtype in (F32, I32) else 1)
        assert self.off + width <= self.n, (name, self.off, width, self.n)
        v = self.back[:, self.off:self.off + width]
        if dtype != BF16:
            v = v.bitcast(dtype)
        if len(shape) > 2:
            v = v.rearrange("p (a b) -> p a b", a=shape[1])
        if shape[0] < 128:
            v = v[0:shape[0]]
        self.off += width
        b = Buf(name, readers=self.pending)
        self.live.append(b)
        return v, b

    def carve_chunks(self, name, n_chunks, tlen, dtype):
        v, b0 = self.carve(name, [128, n_chunks * tlen], dtype)
        v = v.rearrange("p (a b) -> p a b", a=n_chunks)
        self.live.remove(b0)
        bufs = [Buf(f"{name}_{i}", readers=self.pending)
                for i in range(n_chunks)]
        self.live.extend(bufs)
        return v, bufs

    def checkpoint(self):
        return (self.off, len(self.live), list(self.pending))

    def reset_to(self, chk):
        off, nlive, pend = chk
        toks = list(pend)
        for b in self.live[nlive:]:
            if b.w_tok is not None:
                toks.append(b.w_tok)
            toks.extend(b.readers)
        self.pending = toks
        self.live = self.live[:nlive]
        self.off = off

    def reset(self):
        self.reset_to((0, 0, self.pending))


class Ctx:
    def __init__(self, nc, prog, ex):
        self.nc = nc
        self.p = prog
        self.ex = ex

    def sb(self, name, shape, dtype):
        t = self.ex.enter_context(self.nc.sbuf_tensor(name, shape, dtype))
        return t, Buf(name)

    def alloc_psum(self):
        self.psA = self.ex.enter_context(
            self.nc.psum_tensor("psA", [128, 2048], F32))
        self.psB = self.ex.enter_context(
            self.nc.psum_tensor("psB", [128, 1024], F32))
        self.psC = self.ex.enter_context(
            self.nc.psum_tensor("psC", [128, 1024], F32))
        self.psA_lo_b, self.psA_hi_b = Buf("psAlo"), Buf("psAhi")
        self.psB_lo_b, self.psB_hi_b = Buf("psBlo"), Buf("psBhi")
        self.psC_lo_b, self.psC_hi_b = Buf("psClo"), Buf("psChi")


def to_f32c(a):
    return np.ascontiguousarray(np.asarray(a, dtype=np.float32))


def ident_np():
    return np.eye(128, dtype=BF)


def diagmask_np():
    q = np.arange(128)[:, None]
    k = np.arange(128)[None, :]
    return np.where(k <= q, 0.0, NEG).astype(BF)


def yarn_cos_sin_np(seq_len, dim=64, base=10000.0, factor=128.0,
                    beta_fast=32.0, beta_slow=1.0, orig_max=8192):
    def corr_dim(r):
        return dim * math.log(orig_max / (r * 2 * math.pi)) / (2 * math.log(base))
    low = max(math.floor(corr_dim(beta_fast)), 0)
    high = min(math.ceil(corr_dim(beta_slow)), dim - 1)
    hi = high + 0.001 if low == high else float(high)
    ramp = np.clip((np.arange(dim // 2, dtype=np.float32) - low) / (hi - low),
                   0.0, 1.0)
    inv_freq_mask = 1.0 - ramp
    freq_extra = 1.0 / base ** (np.arange(0, dim, 2, dtype=np.float32) / dim)
    freq_inter = freq_extra / factor
    inv_freq = freq_inter * (1.0 - inv_freq_mask) + freq_extra * inv_freq_mask
    t = np.arange(seq_len, dtype=np.float32)
    freqs = np.outer(t, inv_freq)
    emb = np.concatenate([freqs, freqs], axis=-1)
    return np.cos(emb).astype(np.float32), np.sin(emb).astype(np.float32)


def deint_perm(n=64):
    p = np.empty(n, np.int64)
    half = n // 2
    p[:half] = np.arange(half) * 2
    p[half:] = np.arange(half) * 2 + 1
    return p


def host_prep_weights(w):
    out = {}
    out["wqkvT"] = to_f32c(np.asarray(w["L0_attn_w_qkv"]).T)
    out["wdenseT"] = to_f32c(np.asarray(w["L0_attn_w_dense"]).T)
    perm = np.arange(512).reshape(64, 8).T.reshape(-1)
    A_p = np.asarray(w["L0_ffn_A_w"])[perm]
    B_p = np.asarray(w["L0_ffn_B_w"])[:, perm]
    out["A_wT"] = to_f32c(A_p.T)
    out["B_wT"] = to_f32c(B_p.T)
    out["Wq_wT"] = to_f32c(np.asarray(w["L0_ffn_Wq_w"]).T)
    out["keysT"] = to_f32c(np.asarray(w["L0_ffn_keys"]).T)
    wq_b = np.asarray(w["L1_attn_wq_b"])
    pm = deint_perm()
    wq_b_p = wq_b.copy()
    for h in range(32):
        base = h * 128 + 64
        wq_b_p[base:base + 64] = wq_b[base + pm]
    wkv_a = np.asarray(w["L1_attn_wkv_a"])
    wkv_a_p = wkv_a.copy()
    wkv_a_p[256:320] = wkv_a[256 + pm]
    wkv_b3 = np.asarray(w["L1_attn_wkv_b"]).reshape(32, 192, 256)
    out["wq_aT"] = to_f32c(np.asarray(w["L1_attn_wq_a"]).T)
    out["q_ln"] = to_f32c(np.asarray(w["L1_attn_q_a_ln_w"]).reshape(1, 1024))
    out["wq_bT"] = to_f32c(wq_b_p.T)
    out["wkv_aT"] = to_f32c(wkv_a_p.T)
    out["kv_ln"] = to_f32c(np.asarray(w["L1_attn_kv_a_ln_w"]).reshape(1, 256))
    out["wkv_b_kT"] = to_f32c(wkv_b3[:, 0:64, :].reshape(2048, 256).T)
    out["wkv_b_vT"] = to_f32c(wkv_b3[:, 64:192, :].reshape(4096, 256).T)
    out["woT"] = to_f32c(np.asarray(w["L1_attn_wo"]).T)
    cos, sin = yarn_cos_sin_np(T)
    sin_eff = sin.copy()
    sin_eff[:, 0:32] *= -1.0
    out["cosT"] = np.ascontiguousarray(cos.T.astype(BF))
    out["sinT"] = np.ascontiguousarray(sin_eff.T.astype(BF))
    out["qw_gu"] = np.ascontiguousarray(np.asarray(w["L1_ffn_qweight_gu"]))
    out["qz_gu"] = np.ascontiguousarray(np.asarray(w["L1_ffn_qzeros_gu"]))
    out["s_gu"] = to_f32c(w["L1_ffn_scales_gu"])
    out["qw_dn"] = np.ascontiguousarray(np.asarray(w["L1_ffn_qweight_dn"]))
    out["qz_dn"] = np.ascontiguousarray(np.asarray(w["L1_ffn_qzeros_dn"]))
    out["qz_dn"] = np.ascontiguousarray(np.asarray(w["L1_ffn_qzeros_dn"]))
    out["s_dn"] = to_f32c(w["L1_ffn_scales_dn"])
    out["ident"] = ident_np()
    out["diagmask"] = diagmask_np()
    return out


INPUT_SPECS = {
    "x": ([T, D], F32),
    "wqkvT": ([D, 6144], F32),
    "wdenseT": ([D, D], F32),
    "A_wT": ([D, 512], F32),
    "B_wT": ([512, D], F32),
    "Wq_wT": ([D, 16], F32),
    "keysT": ([16, 64], F32),
    "wq_aT": ([D, 1024], F32),
    "q_ln": ([1, 1024], F32),
    "wq_bT": ([1024, D], F32),
    "wkv_aT": ([D, 320], F32),
    "kv_ln": ([1, 256], F32),
    "wkv_b_kT": ([256, 2048], F32),
    "wkv_b_vT": ([256, D], F32),
    "woT": ([D, D], F32),
    "cosT": ([64, T], BF16),
    "sinT": ([64, T], BF16),
    "qw_gu": ([D, 2752], I32),
    "qz_gu": ([32, 2752], I32),
    "s_gu": ([32, 22016], F32),
    "qw_dn": ([11008, 512], I32),
    "qz_dn": ([86, 512], I32),
    "s_dn": ([86, 4096], F32),
    "ident": ([128, 128], BF16),
    "diagmask": ([128, 128], BF16),
}

SCRATCH_SPECS = {
    "hv_sc": ([D, T], BF16),
    "q_sc": ([32, 128, T], BF16),
    "v_sc": ([32, 128, T], BF16),
    "pj_sc": ([32, 16, 128, 128], BF16),
    "x1_sc": ([T, D], F32),
    "x2_sc": ([T, D], F32),
    "x3_sc": ([T, D], F32),
    "wgu_deq": ([32, 128, 22016], BF16),
    "wdn_deq": ([86, 128, 4096], BF16),
    "shl_gu": ([2, 32, 22016], BF16),
    "shl_dn": ([2, 86, 4096], BF16),
    "zs_gu": ([32, 22016], BF16),
    "zs_dn": ([86, 4096], BF16),
    "xns": ([32, T], BF16),
    "hs": ([86, T], BF16),
    "h_sc": ([86, 2, 128, 1024], BF16),
}


def declare_io(nc, fused=True):
    io = {}
    for name, (shape, dt) in INPUT_SPECS.items():
        io[name] = nc.dram_tensor(name, shape, dt, kind="ExternalInput").ap()
        io[name + "__b"] = Buf(name)
    for name, (shape, dt) in SCRATCH_SPECS.items():
        io[name] = nc.dram_tensor(name, shape, dt, kind="Internal").ap()
        io[name + "__b"] = Buf(name)
    io["x4"] = nc.dram_tensor("x4", [T, D], F32, kind="ExternalOutput").ap()
    io["x4__b"] = Buf("x4")
    return io


def alloc_globals(c, io):
    p = c.p
    c.ident, c.ident_b = c.sb("identsb", [128, 128], BF16)
    c.mask, c.mask_b = c.sb("diagmasksb", [128, 128], BF16)
    p.op("sync", lambda e: e.dma_start(out=c.ident[:], in_=io["ident"]),
         writes=[c.ident_b], dma="cst")
    p.op("sync", lambda e: e.dma_start(out=c.mask[:], in_=io["diagmask"]),
         writes=[c.mask_b], dma="cst")
    tok = p.dma_token("cst")
    c.ident_b.w_tok = tok
    c.mask_b.w_tok = tok
    c.lnx = c.sb("lnx", [128, D], F32)
    c.lnn = c.sb("lnn", [128, D], BF16)
    c.lnstats = c.sb("lnstats", [128, 48], F32)
    c.lnmv = c.sb("lnmv", [128, 2], F32)
    c.lnr = c.sb("lnr", [128, 1], F32)
    c.epsc = c.sb("epsc", [128, 1], F32)
    c.eps6 = c.sb("eps6", [128, 1], F32)
    p.op("vector", lambda e: e.memset(c.epsc[0][:], 1e-5), writes=[c.epsc[1]])
    p.op("vector", lambda e: e.memset(c.eps6[0][:], 1e-6), writes=[c.eps6[1]])
    c.onesn = c.sb("onesn", [128, 1], BF16)
    p.op("vector", lambda e: e.memset(c.onesn[0][:], -1.0), writes=[c.onesn[1]])
    c.rowm = c.sb("rowm", [128, 1], F32)
    c.rowm2 = c.sb("rowm2", [128, 1], F32)
    c.negm = c.sb("negm", [128, 4], F32)
    c.negm_bl = [Buf(f"negmc{i}") for i in range(4)]
    c.denom = c.sb("denom", [128, 4], F32)
    c.denom2 = c.sb("denom2", [128, 8], F32)
    c.rden = c.sb("rden", [128, 1], F32)
    c.diag = c.sb("diag", [128, 1024], BF16)
    c.diag_bl = [Buf("diagp0"), Buf("diagp1")]
    c.msq = c.sb("msq", [128, 1], F32)
    c.rrms = c.sb("rrms", [128, 1], F32)
    c.hvq = c.sb("hvq", [128, 512], BF16)
    c.xrow = c.sb("xrow", [1, 1024], BF16)
    c.hrow = c.sb("hrow", [1, 512], BF16)
    c.wst_wf = c.sb("wst_wf", [128, 256], F32)
    c.wst_w0 = c.sb("wst_w0", [128, 128], BF16)


def emit_ln_slab(c, x_d, x_b, s, xT, xT_bufs, eps=1e-5):
    p = c.p
    psB_bf = c.psB[:].bitcast(BF16)
    for tl in range(SLAB // 128):
        tt = s * 8 + tl
        xt, xt_b = c.lnx
        xnt, xnt_b = c.lnn
        p.op("gpsimd", lambda e, tt=tt, xt=xt: e.dma_start(
            out=xt[:], in_=x_d[tt * 128:(tt + 1) * 128, :]),
            reads=[x_b], writes=[xt_b], dma="xl")
        for ch in range(8):
            p.op("vector", lambda e, ch=ch, xt=xt: e.bn_stats(
                out=c.lnstats[0][:, ch * 6:(ch + 1) * 6],
                in_=xt[:, ch * 512:(ch + 1) * 512]),
                reads=[xt_b], writes=[c.lnstats[1]])
        p.op("vector", lambda e: e.bn_aggr(out=c.lnmv[0][:], in_=c.lnstats[0][:]),
             reads=[c.lnstats[1]], writes=[c.lnmv[1]])
        p.op("scalar", lambda e: e.activation(
            out=c.lnr[0][:], in_=c.lnmv[0][:, 1:2], func=AF.Ln,
            bias=c.epsc[0][:]),
            reads=[c.lnmv[1], c.epsc[1]], writes=[c.lnr[1]])
        p.op("scalar", lambda e: e.activation(
            out=c.lnr[0][:], in_=c.lnr[0][:], func=AF.Exp, scale=-0.5),
            reads=[c.lnr[1]], writes=[c.lnr[1]])
        p.op("vector", lambda e, xt=xt, xnt=xnt: e.tensor_scalar(
            out=xnt[:], in0=xt[:], scalar1=c.lnmv[0][:, 0:1],
            scalar2=c.lnr[0][:], op0=ALU.subtract, op1=ALU.mult),
            reads=[xt_b, c.lnmv[1], c.lnr[1]], writes=[xnt_b])
        for g in range(4):
            for j in range(g * 8, g * 8 + 8):
                p.op("tensor", lambda e, j=j, g=g, xnt=xnt: e.transpose(
                    psB_bf[:, (j - g * 8) * 128:(j - g * 8 + 1) * 128],
                    xnt[:, j * 128:(j + 1) * 128], c.ident[:]),
                    reads=[xnt_b, c.ident_b], writes=[c.psB_lo_b])
            src = psB_bf[:, 0:1024].rearrange("p (j t) -> p j t", j=8)
            dst = xT[:, g * 8:g * 8 + 8, tl * 128:(tl + 1) * 128]
            p.op("vector", lambda e, src=src, dst=dst: e.tensor_copy(
                out=dst, in_=src),
                reads=[c.psB_lo_b],
                writes=[xT_bufs[j] for j in range(g * 8, g * 8 + 8)])


def emit_wproj(c, w_d, w_b, actT, actT_bufs, act_tlen, in_chunks, out_tiles,
               evac_cb, engs=("sync", "scalar")):
    assert act_tlen <= 1024
    p = c.p
    if not hasattr(c, "wring"):
        c.wring = [c.sb(f"wring{i}", [128, 256], F32) for i in range(3)]
        c.wringb = [c.sb(f"wringb{i}", [128, 256], BF16) for i in range(3)]
    n_tc = (act_tlen + 511) // 512
    groups = [out_tiles[i:i + 2] for i in range(0, len(out_tiles), 2)]
    ric = 0
    if not hasattr(c, "wproj_pp"):
        c.wproj_pp = 0
    for gi, grp in enumerate(groups):
        pool = c.wproj_pp % 2
        c.wproj_pp += 1
        if pool == 0:
            slots = [(c.psA[:, 0:1024], [c.psA_lo_b]),
                     (c.psA[:, 1024:2048], [c.psA_hi_b])]
        else:
            slots = [(c.psB[:, 0:1024], [c.psB_lo_b, c.psB_hi_b]),
                     (c.psC[:, 0:1024], [c.psC_lo_b, c.psC_hi_b])]
        nw = len(grp) * 128
        for ki, ic in enumerate(in_chunks):
            slot = ric % 3
            ric += 1
            wf, wf_b = c.wring[slot]
            wc, wc_b = c.wringb[slot]
            q = "wsA" if ki % 2 == 0 else "wsB"
            eng = engs[ki % 2]
            p.op(eng, lambda e, ic=ic, g0=grp[0], nw=nw, wf=wf: e.dma_start(
                out=wf[:, 0:nw],
                in_=w_d[ic * 128:(ic + 1) * 128, g0 * 128:g0 * 128 + nw]),
                reads=[w_b], writes=[wf_b], dma=q + str(slot))
            p.op("vector", lambda e, wf=wf, wc=wc, nw=nw: e.tensor_copy(
                out=wc[:, 0:nw], in_=wf[:, 0:nw]),
                reads=[wf_b], writes=[wc_b])
            first = (ki == 0)
            last = (ki == len(in_chunks) - 1)
            for j in range(len(grp)):
                ps, ps_bl = slots[j]
                for tc in range(n_tc):
                    t0, t1 = tc * 512, min((tc + 1) * 512, act_tlen)
                    mov = actT[:, ic, t0:t1]
                    p.op("tensor",
                         lambda e, wc=wc, j=j, mov=mov, ps=ps, t0=t0, t1=t1,
                         first=first, last=last: e.matmul(
                             ps[:, t0:t1], wc[:, j * 128:(j + 1) * 128], mov,
                             start=first, stop=last),
                         reads=[wc_b, actT_bufs[ic]],
                         writes=[ps_bl[(t0 // 512) % len(ps_bl)]])
        for j, ft in enumerate(grp):
            ps, ps_bl = slots[j]
            evac_cb(gi * 2 + j, ps, ps_bl)


def emit_attn_heads(c, R, kT_get, v_get, q_get, scale, hv_dram, hv_b,
                    pbufs, ptb, ptb_bufs, qch_list, q_pair_get=None):
    prog = c.p
    diag2 = c.diag[0].rearrange("p (u i a) -> p u i a", u=2, i=4)
    qpair = None
    cnt = [0]
    pend = [None]

    def flush():
        if pend[0] is None:
            return
        (qh0, qcg, par, v_fn, v_reads, pb_v, pb_b) = pend[0]
        pend[0] = None
        dg_b = c.diag_bl[par]
        n_kt = (qcg + 1) * 4

        def emit_pv(km, last):
            im2 = max(0, km - qcg * 4)
            vsl = v_fn(km)
            prog.op("tensor",
                    lambda e, km=km, im2=im2, vsl=vsl, last=last: e.matmul(
                        c.psC[:, im2 * 128:512], vsl,
                        ptb[:, km % 3, im2 * 128:512],
                        start=(km == 0), stop=last),
                    reads=v_reads + [ptb_bufs[km % 3]], writes=[c.psC_lo_b])
        for kt in range(n_kt):
            i_min = max(0, kt - qcg * 4)
            if kt >= 1:
                emit_pv(kt - 1, False)
            for i in range(i_min, 4):
                prog.op("tensor",
                        lambda e, kt=kt, i=i, pb_v=pb_v, par=par: e.matmul(
                            c.psB[:, i * 128:(i + 1) * 128],
                            pb_v[:, i * T + kt * 128:i * T + (kt + 1) * 128],
                            diag2[:, par, i, :], start=True, stop=True),
                        reads=[pb_b, dg_b], writes=[c.psB_lo_b])
            prog.op("vector",
                    lambda e, kt=kt, i_min=i_min: e.tensor_copy(
                out=ptb[:, kt % 3, i_min * 128:512],
                in_=c.psB[:, i_min * 128:512]),
                reads=[c.psB_lo_b], writes=[ptb_bufs[kt % 3]])
        emit_pv(n_kt - 1, True)
        prog.op("vector", lambda e: e.tensor_copy(
            out=c.hvq[0][:], in_=c.psC[:, 0:512]),
            reads=[c.psC_lo_b], writes=[c.hvq[1]])
        prog.op("gpsimd", lambda e, qh0=qh0, qcg=qcg: e.dma_start(
            out=hv_dram[qh0 * 128:(qh0 + 1) * 128,
                        qcg * 512:(qcg + 1) * 512], in_=c.hvq[0][:]),
            reads=[c.hvq[1]], writes=[hv_b], dma="hvs")

    for qh in range(32):
        if q_pair_get is not None:
            if qh % 2 == 0:
                flush()
                qpair = q_pair_get(qh // 2)
            q_ap, q_b = qpair[qh % 2]
        else:
            flush()
            q_ap, q_b = q_get(qh)
        kT_ap, kT_b = kT_get(qh)
        v_fn, v_reads = v_get(qh)
        for qcg in qch_list:
            par = cnt[0] % 2
            cnt[0] += 1
            pb_v, pb_b = pbufs[par]
            tiles = []
            for i in range(4):
                qt = qcg * 4 + i
                k_len = (qt + 1) * 128
                ql = qt * 128 - (qch_list[0] * 512)
                a_off = 0 if i % 2 == 0 else 1024
                a_buf = c.psA_lo_b if i % 2 == 0 else c.psA_hi_b
                segs = [(c.psA, a_off, min(k_len, 1024), a_buf, 0)]
                rem = k_len - 1024
                if 0 < rem <= 512:
                    if i % 2 == 0:
                        segs.append((c.psB, 512, rem, c.psB_hi_b, 1024))
                    else:
                        segs.append((c.psC, 512, rem, c.psC_hi_b, 1024))
                elif rem > 512:
                    segs.append((c.psB, 512, 512, c.psB_hi_b, 1024))
                    segs.append((c.psC, 512, rem - 512, c.psC_hi_b, 1536))
                tiles.append((i, segs))
                for (ps, off, w, buf, ksrc) in segs:
                    for km in range((w + 511) // 512):
                        k0, k1 = km * 512, min((km + 1) * 512, w)
                        prog.op("tensor",
                                lambda e, ps=ps, off=off, k0=k0, k1=k1, ql=ql,
                                ksrc=ksrc, q_ap=q_ap, kT_ap=kT_ap: e.matmul(
                                    ps[:, off + k0:off + k1],
                                    q_ap[:, ql:ql + 128],
                                    kT_ap[:, ksrc + k0:ksrc + k1],
                                    start=True, stop=True),
                                reads=[q_b, kT_b], writes=[buf])
                mps, moff, mw, mbuf, _ = segs[-1]
                mseg = mps[:, moff + mw - 128:moff + mw]
                prog.op("vector", lambda e, mseg=mseg: e.tensor_tensor(
                    out=mseg, in0=mseg, in1=c.mask[:], op=ALU.add),
                    reads=[c.mask_b, mbuf], writes=[mbuf])
                for si, (ps, off, w, buf, ksrc) in enumerate(segs):
                    if si == 0:
                        prog.op("vector",
                                lambda e, ps=ps, off=off, w=w: e.reduce_max(
                                    out=c.rowm[0][:], in_=ps[:, off:off + w],
                                    axis=AX.X),
                                reads=[buf], writes=[c.rowm[1]])
                    else:
                        prog.op("vector",
                                lambda e, ps=ps, off=off, w=w: e.reduce_max(
                                    out=c.rowm2[0][:], in_=ps[:, off:off + w],
                                    axis=AX.X),
                                reads=[buf], writes=[c.rowm2[1]])
                        prog.op("vector", lambda e: e.tensor_tensor(
                            out=c.rowm[0][:], in0=c.rowm[0][:],
                            in1=c.rowm2[0][:], op=ALU.max),
                            reads=[c.rowm[1], c.rowm2[1]], writes=[c.rowm[1]])
                prog.op("vector", lambda e, i=i: e.tensor_scalar_mul(
                    out=c.negm[0][:, i:i + 1], in0=c.rowm[0][:],
                    scalar1=-scale),
                    reads=[c.rowm[1]], writes=[c.negm_bl[i]])
                for si, (ps, off, w, buf, ksrc) in enumerate(segs):
                    dslot = (c.denom[0][:, i:i + 1] if si == 0 else
                             c.denom2[0][:, (si - 1) * 4 + i:
                                          (si - 1) * 4 + i + 1])
                    dbuf = c.denom[1] if si == 0 else c.denom2[1]
                    prog.op("scalar",
                            lambda e, ps=ps, off=off, w=w, i=i, ksrc=ksrc,
                            pb_v=pb_v, dslot=dslot: e.activation(
                                out=pb_v[:, i * T + ksrc:i * T + ksrc + w],
                                in_=ps[:, off:off + w],
                                func=AF.Exp, scale=scale,
                                bias=c.negm[0][:, i:i + 1],
                                accum_out=dslot),
                            reads=[buf, c.negm_bl[i]],
                            writes=[pb_b, dbuf])
            for (i, segs) in tiles:
                for si in range(1, len(segs)):
                    prog.op("vector", lambda e, i=i, si=si: e.tensor_tensor(
                        out=c.denom[0][:, i:i + 1],
                        in0=c.denom[0][:, i:i + 1],
                        in1=c.denom2[0][:, (si - 1) * 4 + i:
                                         (si - 1) * 4 + i + 1], op=ALU.add),
                        reads=[c.denom[1], c.denom2[1]],
                        writes=[c.denom[1]])
                prog.op("vector", lambda e, i=i: e.reciprocal(
                    out=c.rden[0][:], in_=c.denom[0][:, i:i + 1]),
                    reads=[c.denom[1]], writes=[c.rden[1]])
                prog.op("vector", lambda e, i=i, par=par: e.tensor_scalar_mul(
                    out=diag2[:, par, i, :], in0=c.ident[:],
                    scalar1=c.rden[0][:]),
                    reads=[c.ident_b, c.rden[1]], writes=[c.diag_bl[par]])
            flush()
            pend[0] = (qh, qcg, par, v_fn, v_reads, pb_v, pb_b)
    flush()


def emit_wo_residual(c, R, io, w_key, x_ap, x_b, out_ap, out_b, vstg):
    prog = c.p
    hv_dram, hv_b = io["hv_sc"], io["hv_sc__b"]
    pj_dram, pj_b = io["pj_sc"], io["pj_sc__b"]
    w_d, w_b = io[w_key], io[w_key + "__b"]
    psA_bf = c.psA[:].bitcast(BF16)
    chk = R.checkpoint()
    for h2 in range(2):
        R.reset_to(chk)
        hvT, hvT_bufs = R.carve_chunks(f"hvT{h2}", 32, SLAB, BF16)
        for fch in range(32):
            dst = hvT[:, fch, :]
            prog.op("sync", lambda e, fch=fch, h2=h2, dst=dst: e.dma_start(
                out=dst, in_=hv_dram[fch * 128:(fch + 1) * 128,
                                     h2 * SLAB:(h2 + 1) * SLAB]),
                reads=[hv_b], writes=[hvT_bufs[fch]], dma="hvl")
        tok = prog.dma_token("hvl")
        for b in hvT_bufs:
            b.w_tok = tok

        def oev(j, ps, ps_b, h2=h2):
            vs, vs_b = vstg[j % 2]
            prog.op("vector", lambda e, vs=vs: e.tensor_copy(
                out=vs[:, 0:SLAB], in_=ps[:, 0:SLAB]),
                reads=list(ps_b), writes=[vs_b])
            for tl in range(8):
                dst = pj_dram[j, h2 * 8 + tl]
                prog.op("gpsimd", lambda e, vs=vs, tl=tl, dst=dst: e.dma_start(
                    out=dst, in_=vs[:, tl * 128:(tl + 1) * 128]),
                    reads=[vs_b], writes=[pj_b], dma="pjs%d" % (j % 2))
        emit_wproj(c, w_d, w_b, hvT, hvT_bufs, SLAB,
                   list(range(32)), list(range(32)), oev)
    R.reset_to(chk)
    pjl = [R.carve(f"pjl{i}", [128, 32, 128], BF16) for i in range(2)]
    xr, xr_b = R.carve("xrD", [128, D], F32)
    x1s, x1s_b = c.lnx
    for tt in range(16):
        pj, pj_bb = pjl[tt % 2]
        src = pj_dram[:, tt].rearrange("c p t -> p c t")
        prog.op("gpsimd", lambda e, src=src, pj=pj: e.dma_start(
            out=pj, in_=src),
            reads=[pj_b], writes=[pj_bb], dma="pjl%d" % (tt % 2))
        prog.op("gpsimd", lambda e, tt=tt, xr=xr: e.dma_start(
            out=xr[:], in_=x_ap[tt * 128:(tt + 1) * 128, :]),
            reads=[x_b], writes=[xr_b], dma="xr")
        for och in range(32):
            prog.op("tensor", lambda e, och=och, pj=pj: e.transpose(
                psA_bf[:, och * 128:(och + 1) * 128], pj[:, och, :],
                c.ident[:]),
                reads=[pj_bb, c.ident_b],
                writes=[c.psA_lo_b if och < 16 else c.psA_hi_b])
        prog.op("vector", lambda e, xr=xr: e.tensor_tensor(
            out=x1s[:], in0=psA_bf[:, 0:D], in1=xr[:], op=ALU.add),
            reads=[c.psA_lo_b, c.psA_hi_b, xr_b], writes=[x1s_b])
        prog.op("gpsimd", lambda e, tt=tt: e.dma_start(
            out=out_ap[tt * 128:(tt + 1) * 128, :], in_=x1s[:]),
            reads=[x1s_b], writes=[out_b], dma="x1o")


def emit_l0_attn(c, R, io):
    prog = c.p
    prog.scope = "s1.pre"
    x_d, x_b = io["x"], io["x__b"]
    psB_bf = c.psB[:].bitcast(BF16)
    R.reset()
    kT, kT_bufs = R.carve_chunks("kT", 8, T, BF16)
    vnat, vnat_bufs = R.carve_chunks("vnat", 16, 1024, BF16)
    vstg = [R.carve("vstg0", [128, SLAB], BF16)]
    qhl = [R.carve(f"qTh{i}", [128, SLAB], BF16) for i in range(2)]
    pbv, pb_bl = R.carve_chunks("pbuf", 2, 4 * T, BF16)
    pbufs = [(pbv[:, 0, :], pb_bl[0]), (pbv[:, 1, :], pb_bl[1])]
    ptb, ptb_bufs = R.carve_chunks("ptbuf", 3, 512, BF16)
    chk = R.checkpoint()
    scale = 1.0 / math.sqrt(128.0)

    for s2 in range(2):
        R.reset_to(chk)
        xnT, xnT_bufs = R.carve_chunks(f"s1xnT{s2}", 32, SLAB, BF16)
        emit_ln_slab(c, x_d, x_b, s2, xnT, xnT_bufs)
        prog.scope = "s1.kvproj"

        def kev(j, ps, ps_b, s2=s2):
            dst = kT[:, j, s2 * SLAB:(s2 + 1) * SLAB]
            prog.op("vector", lambda e, dst=dst: e.tensor_copy(
                out=dst, in_=ps[:, 0:SLAB]),
                reads=list(ps_b), writes=[kT_bufs[j]])
        emit_wproj(c, io["wqkvT"], io["wqkvT__b"], xnT, xnT_bufs, SLAB,
                   list(range(32)), list(range(32, 40)), kev)

        def vev(j, ps, ps_b, s2=s2):
            vt, vt_b = vstg[0]
            prog.op("vector", lambda e, vt=vt: e.tensor_copy(
                out=vt[:], in_=ps[:, 0:SLAB]), reads=list(ps_b), writes=[vt_b])
            for tl in range(8):
                prog.op("tensor", lambda e, vt=vt, tl=tl: e.transpose(
                    psB_bf[:, tl * 128:(tl + 1) * 128],
                    vt[:, tl * 128:(tl + 1) * 128], c.ident[:]),
                    reads=[vt_b, c.ident_b], writes=[c.psB_lo_b])
            src = psB_bf[:, 0:1024].rearrange("p (a t) -> p a t", a=8)
            dst = vnat[:, s2 * 8:s2 * 8 + 8, j * 128:(j + 1) * 128]
            prog.op("vector", lambda e, src=src, dst=dst: e.tensor_copy(
                out=dst, in_=src),
                reads=[c.psB_lo_b],
                writes=[vnat_bufs[s2 * 8 + t] for t in range(8)])
        emit_wproj(c, io["wqkvT"], io["wqkvT__b"], xnT, xnT_bufs, SLAB,
                   list(range(32)), list(range(40, 48)), vev)

        def q_pair_get(qp, s2=s2, xnT=xnT, xnT_bufs=xnT_bufs):
            def qev(j, ps, ps_b):
                qt_s, qt_sb = qhl[j]
                prog.op("vector", lambda e, qt_s=qt_s, qt_sb=qt_sb, ps=ps,
                        ps_b=ps_b: e.tensor_copy(
                    out=qt_s[:], in_=ps[:, 0:SLAB]),
                    reads=list(ps_b), writes=[qt_sb])
            emit_wproj(c, io["wqkvT"], io["wqkvT__b"], xnT, xnT_bufs, SLAB,
                       list(range(32)), [qp * 2, qp * 2 + 1], qev)
            return [qhl[0], qhl[1]]

        def kT_get(qh):
            return kT[:, qh // 4, :], kT_bufs[qh // 4]

        def v_get(qh):
            kvh = qh // 4
            return (lambda kt: vnat[:, kt, kvh * 128:(kvh + 1) * 128],
                    [vnat_bufs[k] for k in range(16)])
        prog.scope = "s1.attn"
        emit_attn_heads(c, R, kT_get, v_get, None, scale,
                        io["hv_sc"], io["hv_sc__b"],
                        pbufs, ptb, ptb_bufs, [s2 * 2, s2 * 2 + 1],
                        q_pair_get=q_pair_get)

    R.reset()
    prog.scope = "s1.wo"
    vstg2 = [R.carve(f"ovs{i}", [128, SLAB], BF16) for i in range(2)]
    emit_wo_residual(c, R, io, "wdenseT", x_d, x_b,
                     io["x1_sc"], io["x1_sc__b"], vstg2)


def emit_l0_ffn(c, R, io):
    prog = c.p
    prog.scope = "s2"
    x_d, x_b = io["x1_sc"], io["x1_sc__b"]
    out_d, out_b = io["x2_sc"], io["x2_sc__b"]
    psB_bf = c.psB[:].bitcast(BF16)
    R.reset()
    xnT, xnT_bufs = R.carve_chunks("s2xnT", 32, T, BF16)
    zg, zg_bufs = R.carve_chunks("zg", 4, T, BF16)
    bwb, bwb_bufs = R.carve_chunks("bwb", 4, 1024, BF16)
    gateT, gateT_b = R.carve("gateT", [64, T], BF16)
    qrb, qrb_b = R.carve("qrb", [128, T], BF16)
    wqb, wqb_b = R.carve("wqb2", [128, 512], BF16)
    keysb, keysb_b = R.carve("keysb", [16, 64], BF16)
    scf, scf_b = R.carve("scf", [128, 64], F32)
    m8a, m8a_b = R.carve("m8a", [128, 8], F32)
    m8b, m8b_b = R.carve("m8b", [128, 8], F32)
    rep, rep_b = R.carve("rep", [128, 64], F32)
    negm1, negm1_b = R.carve("negm1", [128, 1], F32)
    pexp, pexp_b = R.carve("pexp", [128, 64], F32)
    msk, msk_b = R.carve("msk", [128, 64], F32)
    zsum, zsum_b = R.carve("zsum", [128, 1], F32)
    rz, rz_b = R.carve("rz", [128, 1], F32)
    gat, gat_b = R.carve("gat", [128, 64], BF16)
    x2s, x2s_b = R.carve("bx2", [128, 2048], F32)

    for s2 in range(2):
        emit_ln_slab(c, x_d, x_b, s2,
                     xnT[:, :, s2 * SLAB:(s2 + 1) * SLAB], xnT_bufs)
    wf, wf_b = c.wst_wf
    for hlf in range(2):
        prog.op("sync", lambda e, hlf=hlf: e.dma_start(
            out=wf[:].rearrange("p (c f) -> p c f", c=16),
            in_=io["Wq_wT"].rearrange("(a c p) f -> a p c f", a=2, p=128)[hlf]),
            reads=[io["Wq_wT__b"]], writes=[wf_b], dma="wst0")
        prog.op("vector", lambda e, hlf=hlf: e.tensor_copy(
            out=wqb[:, hlf * 256:(hlf + 1) * 256], in_=wf[:]),
            reads=[wf_b], writes=[wqb_b])
    prog.op("sync", lambda e: e.dma_start(out=wf[0:16, 0:64], in_=io["keysT"]),
            reads=[io["keysT__b"]], writes=[wf_b], dma="wst0")
    prog.op("vector", lambda e: e.tensor_copy(out=keysb[:], in_=wf[0:16, 0:64]),
            reads=[wf_b], writes=[keysb_b])
    for e_ in range(32):
        for tc in range(4):
            prog.op("tensor", lambda e, e_=e_, tc=tc: e.matmul(
                c.psA[0:16, tc * 512:(tc + 1) * 512],
                wqb[:, e_ * 16:(e_ + 1) * 16],
                xnT[:, e_, tc * 512:(tc + 1) * 512],
                start=(e_ == 0), stop=(e_ == 31)),
                reads=[wqb_b, xnT_bufs[e_]],
                writes=[c.psA_lo_b if tc < 2 else c.psA_hi_b])
    prog.op("vector", lambda e: e.tensor_copy(
        out=qrb[0:16, :], in_=c.psA[0:16, :]),
        reads=[c.psA_lo_b, c.psA_hi_b], writes=[qrb_b])
    for tt in range(16):
        prog.op("tensor", lambda e, tt=tt: e.matmul(
            c.psC[:, 0:64], qrb[0:16, tt * 128:(tt + 1) * 128],
            keysb[:], start=True, stop=True),
            reads=[qrb_b, keysb_b], writes=[c.psC_lo_b])
        prog.op("vector", lambda e: e.tensor_copy(
            out=scf[:], in_=c.psC[:, 0:64]),
            reads=[c.psC_lo_b], writes=[scf_b])
        prog.op("vector", lambda e: e.max(out=m8a[:], in_=scf[:]),
                reads=[scf_b], writes=[m8a_b])
        prog.op("vector", lambda e: e.match_replace(
            out=rep[:], in_to_replace=m8a[:], in_values=scf[:],
            imm_value=NEG), reads=[scf_b, m8a_b], writes=[rep_b])
        prog.op("vector", lambda e: e.max(out=m8b[:], in_=rep[:]),
                reads=[rep_b], writes=[m8b_b])
        prog.op("vector", lambda e: e.tensor_scalar_mul(
            out=negm1[:], in0=m8a[:, 0:1], scalar1=-1.0),
            reads=[m8a_b], writes=[negm1_b])
        prog.op("scalar", lambda e: e.activation(
            out=pexp[:], in_=scf[:], func=AF.Exp, bias=negm1[:]),
            reads=[scf_b, negm1_b], writes=[pexp_b])
        prog.op("vector", lambda e: e.tensor_single_scalar(
            out=msk[:], in_=scf[:], scalar=m8b[:, 7:8], op=ALU.is_ge),
            reads=[scf_b, m8b_b], writes=[msk_b])
        prog.op("vector", lambda e: e.tensor_mul(
            out=pexp[:], in0=pexp[:], in1=msk[:]),
            reads=[pexp_b, msk_b], writes=[pexp_b])
        prog.op("vector", lambda e: e.reduce_sum(
            out=zsum[:], in_=pexp[:], axis=AX.X),
            reads=[pexp_b], writes=[zsum_b])
        prog.op("vector", lambda e: e.reciprocal(out=rz[:], in_=zsum[:]),
                reads=[zsum_b], writes=[rz_b])
        prog.op("vector", lambda e: e.tensor_scalar(
            out=gat[:], in0=pexp[:], scalar1=rz[:], scalar2=2.0,
            op0=ALU.mult, op1=ALU.mult),
            reads=[pexp_b, rz_b], writes=[gat_b])
        prog.op("tensor", lambda e: e.transpose(
            psB_bf[0:64, 0:128], gat[:], c.ident[:]),
            reads=[gat_b, c.ident_b], writes=[c.psB_lo_b])
        prog.op("vector", lambda e, tt=tt: e.tensor_copy(
            out=gateT[:, tt * 128:(tt + 1) * 128],
            in_=psB_bf[0:64, 0:128]),
            reads=[c.psB_lo_b], writes=[gateT_b])

    for s2 in range(2):
        def aev(j, ps, ps_b, s2=s2):
            t0 = s2 * SLAB
            prog.op("vector", lambda e, j=j, t0=t0: e.tensor_tensor(
                out=zg[0:64, j, t0:t0 + SLAB], in0=ps[0:64, 0:SLAB],
                in1=gateT[:, t0:t0 + SLAB], op=ALU.mult),
                reads=list(ps_b) + [gateT_b], writes=[zg_bufs[j]])
            prog.op("vector", lambda e, j=j, t0=t0: e.tensor_tensor(
                out=zg[64:128, j, t0:t0 + SLAB], in0=ps[64:128, 0:SLAB],
                in1=gateT[:, t0:t0 + SLAB], op=ALU.mult),
                reads=list(ps_b) + [gateT_b], writes=[zg_bufs[j]])
        emit_wproj(c, io["A_wT"], io["A_wT__b"],
                   xnT[:, :, s2 * SLAB:(s2 + 1) * SLAB], xnT_bufs, SLAB,
                   list(range(32)), list(range(4)), aev)

    for oh in range(4):
        for ch in range(4):
            prog.op("sync", lambda e, ch=ch, oh=oh: e.dma_start(
                out=c.lnn[0][:].bitcast(F32)[:, 0:1024],
                in_=io["B_wT"][ch * 128:(ch + 1) * 128,
                               oh * 1024:(oh + 1) * 1024]),
                reads=[io["B_wT__b"]], writes=[c.lnn[1]], dma="bw")
            prog.op("vector", lambda e, ch=ch: e.tensor_copy(
                out=bwb[:, ch, :],
                in_=c.lnn[0][:].bitcast(F32)[:, 0:1024]),
                reads=[c.lnn[1]], writes=[bwb_bufs[ch]])
        for tt in range(16):
            xr, xr_b = c.lnx
            prog.op("gpsimd", lambda e, tt=tt, oh=oh, xr=xr: e.dma_start(
                out=xr[:, 0:1024],
                in_=x_d[tt * 128:(tt + 1) * 128, oh * 1024:(oh + 1) * 1024]),
                reads=[x_b], writes=[xr_b], dma="xr")
            for nch in range(4):
                stat = zg[:, nch, tt * 128:(tt + 1) * 128]
                for oc in range(2):
                    mov = bwb[:, nch, oc * 512:(oc + 1) * 512]
                    dstp = c.psA[:, oc * 512:(oc + 1) * 512]
                    prog.op("tensor",
                            lambda e, stat=stat, mov=mov, dstp=dstp, nch=nch:
                            e.matmul(dstp, stat, mov, start=(nch == 0),
                                     stop=(nch == 3)),
                            reads=[zg_bufs[nch], bwb_bufs[nch]],
                            writes=[c.psA_lo_b])
            prog.op("vector", lambda e, xr=xr: e.tensor_tensor(
                out=x2s[:, 0:1024], in0=c.psA[:, 0:1024], in1=xr[:, 0:1024],
                op=ALU.add),
                reads=[c.psA_lo_b, xr_b], writes=[x2s_b])
            prog.op("gpsimd", lambda e, tt=tt, oh=oh: e.dma_start(
                out=out_d[tt * 128:(tt + 1) * 128,
                          oh * 1024:(oh + 1) * 1024], in_=x2s[:, 0:1024]),
                reads=[x2s_b], writes=[out_b], dma="x1o")


def emit_l1_attn(c, R, io):
    prog = c.p
    prog.scope = "s3.pre"
    x_d, x_b = io["x2_sc"], io["x2_sc__b"]
    psB_bf = c.psB[:].bitcast(BF16)
    m_ = 0.1 * math.log(128.0) + 1.0
    scale = (128.0 ** -0.5) * m_ * m_
    R.reset()
    ckvT, ckvT_bufs = R.carve_chunks("ckvT", 2, T, BF16)
    kpeT, kpeT_b = R.carve("kpeT", [64, T], BF16)
    qanT, qanT_bufs = R.carve_chunks("qanT", 8, T, BF16)
    qh_l = [R.carve(f"l1qh{i}", [128, T], BF16) for i in range(2)]
    vh_l = [R.carve(f"l1vh{i}", [128, T], BF16) for i in range(2)]
    chkA = R.checkpoint()
    cosT_s, cosT_b = R.carve("cosTs", [64, T], BF16)
    sinT_s, sinT_b = R.carve("sinTs", [64, T], BF16)
    rp, rp_b = R.carve("rp", [64, T], BF16)
    t2, t2_b = R.carve("t2", [64, T], BF16)
    o1, o1_b = R.carve("o1", [64, T], BF16)
    qln_s, qln_b = R.carve("qlns", [128, 1024], BF16)
    kvln_s, kvln_b = R.carve("kvlns", [128, 256], BF16)
    chk1 = R.checkpoint()

    prog.op("sync", lambda e: e.dma_start(
        out=c.lnx[0][:, 0:1024], in_=io["q_ln"].broadcast_to([128, 1024])),
        reads=[io["q_ln__b"]], writes=[c.lnx[1]], dma="cst2")
    prog.op("vector", lambda e: e.tensor_copy(
        out=qln_s[:], in_=c.lnx[0][:, 0:1024]),
        reads=[c.lnx[1]], writes=[qln_b])
    prog.op("sync", lambda e: e.dma_start(
        out=c.lnx[0][:, 0:256], in_=io["kv_ln"].broadcast_to([128, 256])),
        reads=[io["kv_ln__b"]], writes=[c.lnx[1]], dma="cst3")
    prog.op("vector", lambda e: e.tensor_copy(
        out=kvln_s[:], in_=c.lnx[0][:, 0:256]),
        reads=[c.lnx[1]], writes=[kvln_b])
    prog.op("sync", lambda e: e.dma_start(out=cosT_s[:], in_=io["cosT"]),
            reads=[io["cosT__b"]], writes=[cosT_b], dma="cst2")
    prog.op("sync", lambda e: e.dma_start(out=sinT_s[:], in_=io["sinT"]),
            reads=[io["sinT__b"]], writes=[sinT_b], dma="cst2")
    tok = prog.dma_token("cst2")
    for b in (qln_b, kvln_b, cosT_b, sinT_b):
        b.w_tok = tok

    def rms_pass(srcv, src_bufs, nch, gain_s, gain_b, dstv, dst_bufs, t_off,
                 nat, nat_b):
        for tl in range(SLAB // 128):
            for j in range(nch):
                prog.op("tensor", lambda e, j=j, tl=tl, srcv=srcv: e.transpose(
                    psB_bf[:, j * 128:(j + 1) * 128],
                    srcv[:, j, tl * 128:(tl + 1) * 128], c.ident[:]),
                    reads=[src_bufs[j], c.ident_b], writes=[c.psB_lo_b])
            src = psB_bf[:, 0:nch * 128].rearrange("p (j t) -> p j t", j=nch)
            prog.op("vector", lambda e, src=src, nat=nat, nch=nch:
                    e.tensor_copy(
                        out=nat[:, 0:nch * 128].rearrange(
                            "p (j t) -> p j t", j=nch),
                        in_=src),
                    reads=[c.psB_lo_b], writes=[nat_b])
            nst = max(1, nch * 128 // 512)
            w = min(512, nch * 128)
            for ch in range(nst):
                prog.op("vector", lambda e, ch=ch, w=w, nat=nat: e.bn_stats(
                    out=c.lnstats[0][:, ch * 6:(ch + 1) * 6],
                    in_=nat[:, ch * w:(ch + 1) * w]),
                    reads=[nat_b], writes=[c.lnstats[1]])
            prog.op("vector", lambda e, nst=nst: e.bn_aggr(
                out=c.lnmv[0][:], in_=c.lnstats[0][:, 0:nst * 6]),
                reads=[c.lnstats[1]], writes=[c.lnmv[1]])
            prog.op("vector", lambda e: e.tensor_tensor(
                out=c.msq[0][:], in0=c.lnmv[0][:, 0:1],
                in1=c.lnmv[0][:, 0:1], op=ALU.mult),
                reads=[c.lnmv[1]], writes=[c.msq[1]])
            prog.op("vector", lambda e: e.tensor_tensor(
                out=c.msq[0][:], in0=c.msq[0][:], in1=c.lnmv[0][:, 1:2],
                op=ALU.add), reads=[c.msq[1], c.lnmv[1]], writes=[c.msq[1]])
            prog.op("scalar", lambda e: e.activation(
                out=c.rrms[0][:], in_=c.msq[0][:], func=AF.Ln,
                bias=c.eps6[0][:]),
                reads=[c.msq[1], c.eps6[1]], writes=[c.rrms[1]])
            prog.op("scalar", lambda e: e.activation(
                out=c.rrms[0][:], in_=c.rrms[0][:], func=AF.Exp, scale=-0.5),
                reads=[c.rrms[1]], writes=[c.rrms[1]])
            prog.op("vector", lambda e, nat=nat, nch=nch: e.tensor_scalar_mul(
                out=nat[:, 0:nch * 128], in0=nat[:, 0:nch * 128],
                scalar1=c.rrms[0][:]),
                reads=[nat_b, c.rrms[1]], writes=[nat_b])
            prog.op("vector", lambda e, nat=nat, nch=nch, gain_s=gain_s:
                    e.tensor_mul(
                        out=nat[:, 0:nch * 128], in0=nat[:, 0:nch * 128],
                        in1=gain_s[:, 0:nch * 128]),
                    reads=[nat_b, gain_b], writes=[nat_b])
            for j in range(nch):
                prog.op("tensor", lambda e, j=j, nat=nat: e.transpose(
                    psB_bf[:, j * 128:(j + 1) * 128],
                    nat[:, j * 128:(j + 1) * 128], c.ident[:]),
                    reads=[nat_b, c.ident_b], writes=[c.psB_lo_b])
            src2 = psB_bf[:, 0:nch * 128].rearrange("p (j t) -> p j t", j=nch)
            dst = dstv[:, :, t_off + tl * 128:t_off + (tl + 1) * 128]
            prog.op("vector", lambda e, src2=src2, dst=dst: e.tensor_copy(
                out=dst, in_=src2), reads=[c.psB_lo_b], writes=dst_bufs)

    for s2 in range(2):
        R.reset_to(chk1)
        xnT, xnT_bufs = R.carve_chunks(f"s3xnT{s2}", 32, SLAB, BF16)
        qaT, qaT_bufs = R.carve_chunks(f"qaT{s2}", 8, SLAB, BF16)
        ckaT, ckaT_bufs = R.carve_chunks(f"ckaT{s2}", 2, SLAB, BF16)
        qa_nat = R.carve(f"qanat{s2}", [128, 1024], BF16)
        ck_nat = R.carve(f"cknat{s2}", [128, 256], BF16)
        emit_ln_slab(c, x_d, x_b, s2, xnT, xnT_bufs)
        prog.scope = "s3.aproj"

        def qaev(j, ps, ps_b, qaT=qaT, qaT_bufs=qaT_bufs):
            dst = qaT[:, j, :]
            prog.op("vector", lambda e, dst=dst, j=j: e.tensor_copy(
                out=dst, in_=ps[:, 0:SLAB]),
                reads=list(ps_b), writes=[qaT_bufs[j]])
        emit_wproj(c, io["wq_aT"], io["wq_aT__b"], xnT, xnT_bufs, SLAB,
                   list(range(32)), list(range(8)), qaev)

        def ckev(j, ps, ps_b, ckaT=ckaT, ckaT_bufs=ckaT_bufs):
            dst = ckaT[:, j, :]
            prog.op("vector", lambda e, dst=dst, j=j: e.tensor_copy(
                out=dst, in_=ps[:, 0:SLAB]),
                reads=list(ps_b), writes=[ckaT_bufs[j]])
        emit_wproj(c, io["wkv_aT"], io["wkv_aT__b"], xnT, xnT_bufs, SLAB,
                   list(range(32)), list(range(2)), ckev)

        wf, wf_b = c.wst_wf
        w0, w0_b = c.wst_w0
        for ki in range(32):
            prog.op("sync", lambda e, ki=ki: e.dma_start(
                out=wf[:, 0:64],
                in_=io["wkv_aT"][ki * 128:(ki + 1) * 128, 256:320]),
                reads=[io["wkv_aT__b"]], writes=[wf_b], dma="wst0")
            prog.op("vector", lambda e: e.tensor_copy(
                out=w0[:, 0:64], in_=wf[:, 0:64]),
                reads=[wf_b], writes=[w0_b])
            for tc2 in range(2):
                prog.op("tensor", lambda e, ki=ki, tc2=tc2, xnT=xnT: e.matmul(
                    c.psC[0:64, tc2 * 512:(tc2 + 1) * 512], w0[:, 0:64],
                    xnT[:, ki, tc2 * 512:(tc2 + 1) * 512],
                    start=(ki == 0), stop=(ki == 31)),
                    reads=[w0_b, xnT_bufs[ki]],
                    writes=[c.psC_lo_b if tc2 == 0 else c.psC_hi_b])
        s_off = s2 * SLAB
        prog.op("vector", lambda e: e.tensor_copy(
            out=rp[:, 0:SLAB], in_=c.psC[0:64, 0:SLAB]),
            reads=[c.psC_lo_b, c.psC_hi_b], writes=[rp_b])
        prog.op("vector", lambda e: e.tensor_copy(
            out=t2[0:32, 0:SLAB], in_=rp[32:64, 0:SLAB]),
            reads=[rp_b], writes=[t2_b])
        prog.op("vector", lambda e: e.tensor_copy(
            out=t2[32:64, 0:SLAB], in_=rp[0:32, 0:SLAB]),
            reads=[rp_b], writes=[t2_b])
        prog.op("vector", lambda e, s_off=s_off: e.tensor_mul(
            out=o1[:, 0:SLAB], in0=rp[:, 0:SLAB],
            in1=cosT_s[:, s_off:s_off + SLAB]),
            reads=[rp_b, cosT_b], writes=[o1_b])
        prog.op("vector", lambda e, s_off=s_off: e.tensor_mul(
            out=t2[:, 0:SLAB], in0=t2[:, 0:SLAB],
            in1=sinT_s[:, s_off:s_off + SLAB]),
            reads=[t2_b, sinT_b], writes=[t2_b])
        prog.op("vector", lambda e, s_off=s_off: e.tensor_tensor(
            out=kpeT[:, s_off:s_off + SLAB], in0=o1[:, 0:SLAB],
            in1=t2[:, 0:SLAB], op=ALU.add),
            reads=[o1_b, t2_b], writes=[kpeT_b])

        rms_pass(qaT, qaT_bufs, 8, qln_s, qln_b, qanT, qanT_bufs, s_off,
                 *qa_nat)
        rms_pass(ckaT, ckaT_bufs, 2, kvln_s, kvln_b, ckvT, ckvT_bufs, s_off,
                 *ck_nat)

    prog.scope = "s3.wqb"
    R.reset_to(chk1)
    for s2 in range(2):
        t0q = s2 * SLAB

        def qev(j, ps, ps_b, t0q=t0q):
            qs, qs_b = qh_l[j % 2]
            prog.op("vector", lambda e, qs=qs: e.tensor_copy(
                out=qs[:, 0:SLAB], in_=ps[:, 0:SLAB]),
                reads=list(ps_b), writes=[qs_b])
            prog.op("vector", lambda e, qs=qs: e.tensor_copy(
                out=rp[:, 0:SLAB], in_=qs[64:128, 0:SLAB]),
                reads=[qs_b], writes=[rp_b])
            prog.op("vector", lambda e: e.tensor_copy(
                out=t2[0:32, 0:SLAB], in_=rp[32:64, 0:SLAB]),
                reads=[rp_b], writes=[t2_b])
            prog.op("vector", lambda e: e.tensor_copy(
                out=t2[32:64, 0:SLAB], in_=rp[0:32, 0:SLAB]),
                reads=[rp_b], writes=[t2_b])
            prog.op("vector", lambda e, t0q=t0q: e.tensor_mul(
                out=o1[:, 0:SLAB], in0=rp[:, 0:SLAB],
                in1=cosT_s[:, t0q:t0q + SLAB]),
                reads=[rp_b, cosT_b], writes=[o1_b])
            prog.op("vector", lambda e, t0q=t0q: e.tensor_mul(
                out=t2[:, 0:SLAB], in0=t2[:, 0:SLAB],
                in1=sinT_s[:, t0q:t0q + SLAB]),
                reads=[t2_b, sinT_b], writes=[t2_b])
            prog.op("vector", lambda e, qs=qs: e.tensor_tensor(
                out=qs[64:128, 0:SLAB], in0=o1[:, 0:SLAB],
                in1=t2[:, 0:SLAB], op=ALU.add),
                reads=[o1_b, t2_b], writes=[qs_b])
            prog.op("gpsimd", lambda e, j=j, qs=qs, t0q=t0q: e.dma_start(
                out=io["q_sc"][j][:, t0q:t0q + SLAB], in_=qs[:, 0:SLAB]),
                reads=[qs_b], writes=[io["q_sc__b"]], dma="qs%d" % (j % 2))
        emit_wproj(c, io["wq_bT"], io["wq_bT__b"],
                   qanT[:, :, t0q:t0q + SLAB], qanT_bufs, SLAB,
                   list(range(8)), list(range(32)), qev)

    prog.scope = "s3.kv"
    R.reset_to(chkA)
    knT, knT_bufs = R.carve_chunks("knT", 16, T, BF16)
    kf, kf_b = R.carve("kf", [128, T], BF16)
    vhn, vhn_b = R.carve("vhn", [128, 16 * 128], BF16)
    vhnv = vhn.rearrange("p (d t) -> p d t", d=16)
    pbv, pb_bl = R.carve_chunks("l1pbuf", 2, 4 * T, BF16)
    pbufs = [(pbv[:, 0, :], pb_bl[0]), (pbv[:, 1, :], pb_bl[1])]
    ptb, ptb_bufs = R.carve_chunks("l1ptbuf", 3, 512, BF16)

    for s2 in range(2):
        t0q = s2 * SLAB

        def kev(j, ps, ps_b, t0q=t0q):
            dst = knT[:, j, t0q:t0q + SLAB]
            prog.op("vector", lambda e, dst=dst, j=j: e.tensor_copy(
                out=dst, in_=ps[:, 0:SLAB]), reads=list(ps_b),
                writes=[knT_bufs[j]])
        emit_wproj(c, io["wkv_b_kT"], io["wkv_b_kT__b"],
                   ckvT[:, :, t0q:t0q + SLAB], ckvT_bufs, SLAB,
                   list(range(2)), list(range(16)), kev)

        def vev(j, ps, ps_b, t0q=t0q):
            vs, vs_b = vh_l[j % 2]
            prog.op("vector", lambda e, vs=vs: e.tensor_copy(
                out=vs[:, 0:SLAB], in_=ps[:, 0:SLAB]),
                reads=list(ps_b), writes=[vs_b])
            prog.op("gpsimd", lambda e, j=j, vs=vs, t0q=t0q: e.dma_start(
                out=io["v_sc"][j][:, t0q:t0q + SLAB], in_=vs[:, 0:SLAB]),
                reads=[vs_b], writes=[io["v_sc__b"]], dma="vs%d" % (j % 2))
        emit_wproj(c, io["wkv_b_vT"], io["wkv_b_vT__b"],
                   ckvT[:, :, t0q:t0q + SLAB], ckvT_bufs, SLAB,
                   list(range(2)), list(range(32)), vev)

    def q_get(qh):
        qt_s, qt_sb = qh_l[qh % 2]
        prog.op("gpsimd", lambda e, qh=qh, qt_s=qt_s: e.dma_start(
            out=qt_s[:], in_=io["q_sc"][qh]),
            reads=[io["q_sc__b"]], writes=[qt_sb], dma="ql%d" % (qh % 2))
        return qt_s, qt_sb

    def kT_get(qh):
        kno = knT[:, qh // 2, :]
        prog.op("vector", lambda e, qh=qh, kno=kno: e.tensor_copy(
            out=kf[0:64, :], in_=kno[(qh % 2) * 64:(qh % 2) * 64 + 64, :]),
            reads=[knT_bufs[qh // 2]], writes=[kf_b])
        prog.op("vector", lambda e: e.tensor_copy(
            out=kf[64:128, :], in_=kpeT[:]),
            reads=[kpeT_b], writes=[kf_b])
        return kf, kf_b

    def v_get(qh):
        vt_s, vt_sb = vh_l[qh % 2]
        prog.op("gpsimd", lambda e, qh=qh, vt_s=vt_s: e.dma_start(
            out=vt_s[:], in_=io["v_sc"][qh]),
            reads=[io["v_sc__b"]], writes=[vt_sb], dma="vl%d" % (qh % 2))
        for g8 in range(2):
            for tl2 in range(8):
                prog.op("tensor", lambda e, g8=g8, tl2=tl2, vt_s=vt_s:
                        e.transpose(
                            psB_bf[:, tl2 * 128:(tl2 + 1) * 128],
                            vt_s[:, (g8 * 8 + tl2) * 128:
                                 (g8 * 8 + tl2 + 1) * 128], c.ident[:]),
                        reads=[vt_sb, c.ident_b], writes=[c.psB_lo_b])
            src = psB_bf[:, 0:1024].rearrange("p (j t) -> p j t", j=8)
            dst = vhnv[:, g8 * 8:(g8 + 1) * 8, :]
            prog.op("vector", lambda e, src=src, dst=dst: e.tensor_copy(
                out=dst, in_=src), reads=[c.psB_lo_b], writes=[vhn_b])
        return (lambda kt: vhnv[:, kt, :]), [vhn_b]

    prog.scope = "s3.attn"
    emit_attn_heads(c, R, kT_get, v_get, q_get, scale,
                    io["hv_sc"], io["hv_sc__b"],
                    pbufs, ptb, ptb_bufs, [0, 1, 2, 3])

    prog.scope = "s3.wo"
    R.reset_to(chkA)
    vstg2 = [R.carve(f"l1ovs{i}", [128, SLAB], BF16) for i in range(2)]
    emit_wo_residual(c, R, io, "woT", x_d, x_b,
                     io["x3_sc"], io["x3_sc__b"], vstg2)


def emit_l1_ffn(c, R, io):
    prog = c.p
    prog.scope = "s4.zs"
    x_d, x_b = io["x3_sc"], io["x3_sc__b"]
    zsg_d, zsg_b = io["zs_gu"], io["zs_gu__b"]
    zsd_d, zsd_b = io["zs_dn"], io["zs_dn__b"]
    wgu_d, wgu_b = io["wgu_deq"], io["wgu_deq__b"]
    wdn_d, wdn_b = io["wdn_deq"], io["wdn_deq__b"]
    xns_d, xns_b = io["xns"], io["xns__b"]
    hs_d, hs_b = io["hs"], io["hs__b"]
    hsc_d, hsc_b = io["h_sc"], io["h_sc__b"]
    R.reset()
    xnsT, xnsT_b = R.carve("xnsT", [32, T], BF16)
    hsT, hsT_b = R.carve("hsT", [86, 512], BF16)
    zspan, zspan_b = R.carve("zspan", [32, 1024], BF16)
    zdn, zdn_b = R.carve("zdn", [86, 1024], BF16)
    sg_stage = [R.carve(f"sgst{i}", [128, 1024], BF16) for i in range(2)]
    hstage = [R.carve(f"hst{i}", [128, 1024], BF16) for i in range(2)]
    chk0 = R.checkpoint()

    zw, zw_b = R.carve("zw", [86, 2752], I32)
    zn, zn_b = R.carve("zn", [86, 5504], I32)
    zs_s, zs_s_b = R.carve("zs_s", [86, 5504], F32)
    zs_o, zs_o_b = R.carve("zs_o", [86, 5504], BF16)
    prog.op("sync", lambda e: e.dma_start(out=zw[0:32, :], in_=io["qz_gu"]),
            reads=[io["qz_gu__b"]], writes=[zw_b], dma="zw")
    for cc in range(4):
        for k in range(8):
            prog.op("vector", lambda e, cc=cc, k=k: e.tensor_scalar(
                out=zn[0:32, k::8], in0=zw[0:32, cc * 688:(cc + 1) * 688],
                scalar1=4 * k, scalar2=15,
                op0=ALU.logical_shift_right, op1=ALU.bitwise_and),
                reads=[zw_b], writes=[zn_b])
        prog.op("sync", lambda e, cc=cc: e.dma_start(
            out=zs_s[0:32, :], in_=io["s_gu"][:, cc * 5504:(cc + 1) * 5504]),
            reads=[io["s_gu__b"]], writes=[zs_s_b], dma="zss")
        prog.op("vector", lambda e: e.tensor_tensor(
            out=zs_o[0:32, :], in0=zn[0:32, :], in1=zs_s[0:32, :],
            op=ALU.mult), reads=[zn_b, zs_s_b], writes=[zs_o_b])
        prog.op("gpsimd", lambda e, cc=cc: e.dma_start(
            out=zsg_d[:, cc * 5504:(cc + 1) * 5504], in_=zs_o[0:32, :]),
            reads=[zs_o_b], writes=[zsg_b], dma="zso")
    prog.op("sync", lambda e: e.dma_start(out=zw[0:86, 0:512], in_=io["qz_dn"]),
            reads=[io["qz_dn__b"]], writes=[zw_b], dma="zw")
    for k in range(8):
        prog.op("vector", lambda e, k=k: e.tensor_scalar(
            out=zn[0:86, k:4096:8], in0=zw[0:86, 0:512],
            scalar1=4 * k, scalar2=15,
            op0=ALU.logical_shift_right, op1=ALU.bitwise_and),
            reads=[zw_b], writes=[zn_b])
    prog.op("sync", lambda e: e.dma_start(
        out=zs_s[0:86, 0:4096], in_=io["s_dn"]),
        reads=[io["s_dn__b"]], writes=[zs_s_b], dma="zss")
    prog.op("vector", lambda e: e.tensor_tensor(
        out=zs_o[0:86, 0:4096], in0=zn[0:86, 0:4096],
        in1=zs_s[0:86, 0:4096], op=ALU.mult),
        reads=[zn_b, zs_s_b], writes=[zs_o_b])
    prog.op("gpsimd", lambda e: e.dma_start(
        out=zsd_d, in_=zs_o[0:86, 0:4096]),
        reads=[zs_o_b], writes=[zsd_b], dma="zso")

    prog.scope = "s4.deq"
    shl_g, shl_g_b = io["shl_gu"], io["shl_gu__b"]
    shl_d, shl_d_b = io["shl_dn"], io["shl_dn__b"]
    R.reset_to(chk0)
    sf, sf_b = R.carve("sf", [86, 5504], F32)
    shi, shi_b = R.carve("shi", [86, 5504], BF16)
    shf, shf_b = R.carve("shf", [86, 5504], F32)
    slo, slo_b = R.carve("slo", [86, 5504], BF16)

    def emit_split(nr, w, src_ap, src_b, dst_hi, dst_lo):
        prog.op("sync", lambda e: e.dma_start(out=sf[0:nr, 0:w], in_=src_ap),
                reads=[src_b], writes=[sf_b], dma="shl")
        prog.op("vector", lambda e: e.tensor_copy(
            out=shi[0:nr, 0:w], in_=sf[0:nr, 0:w]),
            reads=[sf_b], writes=[shi_b])
        prog.op("vector", lambda e: e.tensor_copy(
            out=shf[0:nr, 0:w], in_=shi[0:nr, 0:w]),
            reads=[shi_b], writes=[shf_b])
        prog.op("vector", lambda e: e.tensor_tensor(
            out=shf[0:nr, 0:w], in0=sf[0:nr, 0:w], in1=shf[0:nr, 0:w],
            op=ALU.subtract), reads=[sf_b, shf_b], writes=[shf_b])
        prog.op("vector", lambda e: e.tensor_copy(
            out=slo[0:nr, 0:w], in_=shf[0:nr, 0:w]),
            reads=[shf_b], writes=[slo_b])
        prog.op("gpsimd", lambda e, dst_hi=dst_hi: e.dma_start(
            out=dst_hi, in_=shi[0:nr, 0:w]),
            reads=[shi_b], writes=[shl_g_b], dma="shlo")
        prog.op("gpsimd", lambda e, dst_lo=dst_lo: e.dma_start(
            out=dst_lo, in_=slo[0:nr, 0:w]),
            reads=[slo_b], writes=[shl_g_b], dma="shlo")
    for cc in range(4):
        emit_split(32, 5504, io["s_gu"][:, cc * 5504:(cc + 1) * 5504],
                   io["s_gu__b"],
                   shl_g[0][:, cc * 5504:(cc + 1) * 5504],
                   shl_g[1][:, cc * 5504:(cc + 1) * 5504])
    emit_split(86, 4096, io["s_dn"], io["s_dn__b"], shl_d[0], shl_d[1])
    shl_d_b.w_tok = shl_g_b.w_tok = prog.dma_token("shlo")

    R.reset_to(chk0)
    qwl = [R.carve(f"qwrow{i}", [128, 2752], I32) for i in range(2)]
    wil = [R.carve(f"wint{i}", [128, 5504], I32) for i in range(2)]
    srl = [R.carve(f"srl{i}", [1, 11008], BF16) for i in range(2)]
    wdq, wdq_bufs = R.carve_chunks("wdq", 2, 11008, BF16)
    onesr, onesr_b = R.carve("onesr", [1, 128], BF16)
    prog.op("vector", lambda e: e.memset(onesr, 1.0), writes=[onesr_b])
    bslots = [(c.psA, 0, [c.psA_lo_b]), (c.psA, 1024, [c.psA_hi_b]),
              (c.psB, 0, [c.psB_lo_b, c.psB_hi_b]),
              (c.psC, 0, [c.psC_lo_b, c.psC_hi_b])]
    bs_ctr = [0]

    def emit_scaled_mult(wint, wint_b, srow, srow_b, o0, w, dst, dst_b,
                         lo_off):
        ps, off, pbl = bslots[bs_ctr[0] % 4]
        bs_ctr[0] += 1
        for half in range((w + 511) // 512):
            h0, h1 = half * 512, min(half * 512 + 512, w)
            hb = pbl[min(half, len(pbl) - 1)]
            for hl in range(2):
                prog.op("tensor",
                        lambda e, ps=ps, off=off, h0=h0, h1=h1, hl=hl,
                        srow=srow, o0=o0, lo_off=lo_off: e.matmul(
                            ps[:, off + h0:off + h1], onesr,
                            srow[0:1, hl * lo_off + o0 + h0:
                                 hl * lo_off + o0 + h1],
                            start=(hl == 0), stop=(hl == 1)),
                        reads=[onesr_b, srow_b], writes=[hb])
        prog.op("vector", lambda e, ps=ps, off=off, o0=o0, w=w, wint=wint,
                dst=dst: e.tensor_tensor(
            out=dst, in0=wint[:, o0:o0 + w], in1=ps[:, off:off + w],
            op=ALU.mult),
            reads=[wint_b] + pbl, writes=[dst_b])
    DENG = ("sync", "scalar", "gpsimd")
    for g in range(32):
        qwrow, qwrow_b = qwl[g % 2]
        prog.op(DENG[g % 2], lambda e, g=g, qwrow=qwrow: e.dma_start(
            out=qwrow[:], in_=io["qw_gu"][g * 128:(g + 1) * 128, :]),
            reads=[io["qw_gu__b"]], writes=[qwrow_b], dma=f"qw{g % 2}")
        for cc in range(4):
            wint, wint_b = wil[cc % 2]
            srow, srow_b = srl[cc % 2]
            hf = cc // 2
            for k in range(8):
                prog.op("vector", lambda e, cc=cc, k=k, wint=wint,
                        qwrow=qwrow: e.tensor_scalar(
                    out=wint[:, k::8], in0=qwrow[:, cc * 688:(cc + 1) * 688],
                    scalar1=4 * k, scalar2=15,
                    op0=ALU.logical_shift_right, op1=ALU.bitwise_and),
                    reads=[qwrow_b], writes=[wint_b])
            prog.op(DENG[cc % 3], lambda e, g=g, cc=cc, srow=srow:
                    e.dma_start(
                out=srow[0:1, :], in_=shl_g[:, g, cc * 5504:(cc + 1) * 5504]),
                reads=[shl_g_b], writes=[srow_b], dma=f"sbc{cc}")
            for (o0, w) in ((0, 1024), (1024, 1024), (2048, 1024),
                            (3072, 1024), (4096, 1024), (5120, 384)):
                dst = wdq[:, hf, (cc % 2) * 5504 + o0:
                          (cc % 2) * 5504 + o0 + w]
                emit_scaled_mult(wint, wint_b, srow, srow_b, o0, w,
                                 dst, wdq_bufs[hf], 5504)
        for hf in range(2):
            prog.op(DENG[(g + hf) % 3], lambda e, g=g, hf=hf: e.dma_start(
                out=wgu_d[g][:, hf * 11008:(hf + 1) * 11008],
                in_=wdq[:, hf, :]),
                reads=[wdq_bufs[hf]], writes=[wgu_b], dma=f"wdq{(g + hf) % 3}")
    for g in range(86):
        qwrow, qwrow_b = qwl[g % 2]
        wint, wint_b = wil[g % 2]
        srow, srow_b = srl[g % 2]
        prog.op(DENG[g % 2], lambda e, g=g, qwrow=qwrow: e.dma_start(
            out=qwrow[:, 0:512], in_=io["qw_dn"][g * 128:(g + 1) * 128, :]),
            reads=[io["qw_dn__b"]], writes=[qwrow_b], dma=f"qw{g % 2}")
        for k in range(8):
            prog.op("vector", lambda e, k=k, wint=wint, qwrow=qwrow:
                    e.tensor_scalar(
                out=wint[:, k:4096:8], in0=qwrow[:, 0:512],
                scalar1=4 * k, scalar2=15,
                op0=ALU.logical_shift_right, op1=ALU.bitwise_and),
                reads=[qwrow_b], writes=[wint_b])
        prog.op(DENG[g % 3], lambda e, g=g, srow=srow: e.dma_start(
            out=srow[0:1, 0:8192], in_=shl_d[:, g, :]),
            reads=[shl_d_b], writes=[srow_b], dma=f"sbc{g % 3}")
        for (o0, w) in ((0, 1024), (1024, 1024), (2048, 1024), (3072, 1024)):
            dst = wdq[:, g % 2, o0:o0 + w]
            emit_scaled_mult(wint, wint_b, srow, srow_b, o0, w,
                             dst, wdq_bufs[g % 2], 4096)
        prog.op(DENG[g % 3], lambda e, g=g: e.dma_start(
            out=wdn_d[g], in_=wdq[:, g % 2, 0:4096]),
            reads=[wdq_bufs[g % 2]], writes=[wdn_b], dma=f"wdq{g % 3}")

    prog.scope = "s4.gateup"
    ip_groups = [list(range(a, min(a + 2, 86))) for a in range(0, 86, 2)]
    for ts in range(2):
        R.reset_to(chk0)
        xnT, xnT_bufs = R.carve_chunks(f"s4xnT{ts}", 32, SLAB, BF16)
        spr = []
        for i in range(2):
            v, bufs = R.carve_chunks(f"spr{i}", 32, 512, BF16)
            spr.append((v, bufs))
        zspr = [R.carve(f"zspr{i}", [32, 512], BF16) for i in range(2)]
        emit_ln_slab(c, x_d, x_b, ts, xnT, xnT_bufs)
        for g in range(32):
            for tc in range(2):
                prog.op("tensor", lambda e, g=g, tc=tc, xnT=xnT: e.matmul(
                    c.psA[0:1, tc * 512:(tc + 1) * 512], c.onesn[0][:],
                    xnT[:, g, tc * 512:(tc + 1) * 512],
                    start=True, stop=True),
                    reads=[c.onesn[1], xnT_bufs[g]], writes=[c.psA_lo_b])
            prog.op("vector", lambda e: e.tensor_copy(
                out=c.xrow[0][:], in_=c.psA[0:1, 0:1024]),
                reads=[c.psA_lo_b], writes=[c.xrow[1]])
            prog.op("gpsimd", lambda e, g=g, ts=ts: e.dma_start(
                out=xns_d[g:g + 1, ts * SLAB:(ts + 1) * SLAB],
                in_=c.xrow[0][:]),
                reads=[c.xrow[1]], writes=[xns_b], dma="xns")
        prog.op("gpsimd", lambda e, ts=ts: e.dma_start(
            out=xnsT[:, 0:SLAB], in_=xns_d[:, ts * SLAB:(ts + 1) * SLAB]),
            reads=[xns_b], writes=[xnsT_b], dma="xnsl")
        zsg_v = zsg_d.rearrange("g (u n) -> g u n", u=2)
        for gi, ipg in enumerate(ip_groups):
            n_ip = len(ipg)
            ip0 = ipg[0]
            rb = gi % 2
            spw, sp_bufs = spr[rb]
            zs2, zs2_b = zspr[rb]
            prog.op("sync", lambda e, ip0=ip0, n_ip=n_ip, zs2=zs2:
                    e.dma_start(
                        out=zs2[:, 0:2 * n_ip * 128].rearrange(
                            "g (u a) -> g u a", u=2),
                        in_=zsg_v[:, :, ip0 * 128:(ip0 + n_ip) * 128]),
                    reads=[zsg_b], writes=[zs2_b], dma="zsl%d" % rb)
            for hg in range(32):
                eng = DENG[hg % 3]
                q = "spg%d" % (hg % 9)
                src = wgu_d[hg].rearrange(
                    "p (u n) -> p u n", u=2)[:, :, ip0 * 128:(ip0 + n_ip) * 128]
                dst = spw[:, hg, 0:2 * n_ip * 128].rearrange(
                    "p (u a) -> p u a", u=2)
                prog.op(eng, lambda e, src=src, dst=dst: e.dma_start(
                    out=dst, in_=src),
                    reads=[wgu_b], writes=[sp_bufs[hg]], dma=q)
            for i2, ip in enumerate(ipg):
                for gu in range(2):
                    pst, o0 = ((c.psA, i2 * 1024) if gu == 0 else
                               ((c.psB, 0) if i2 == 0 else (c.psC, 0)))
                    zoff = gu * 256 + i2 * 128
                    for tc in range(2):
                        pb = ((c.psA_lo_b if i2 == 0 else c.psA_hi_b)
                              if gu == 0 else
                              ((c.psB_lo_b, c.psB_hi_b)[tc] if i2 == 0
                               else (c.psC_lo_b, c.psC_hi_b)[tc]))
                        prog.op("tensor",
                                lambda e, pst=pst, o0=o0, tc=tc, zoff=zoff,
                                zs2=zs2: e.matmul(
                                    pst[:, o0 + tc * 512:o0 + (tc + 1) * 512],
                                    zs2[:, zoff:zoff + 128],
                                    xnsT[:, tc * 512:(tc + 1) * 512],
                                    start=True, stop=False),
                                reads=[zs2_b, xnsT_b], writes=[pb])
            for hg in range(32):
                for i2, ip in enumerate(ipg):
                    for gu in range(2):
                        pst, o0 = ((c.psA, i2 * 1024) if gu == 0 else
                                   ((c.psB, 0) if i2 == 0 else (c.psC, 0)))
                        soff = gu * 256 + i2 * 128
                        last = (hg == 31)
                        for tc in range(2):
                            pb = ((c.psA_lo_b if i2 == 0 else c.psA_hi_b)
                                  if gu == 0 else
                                  ((c.psB_lo_b, c.psB_hi_b)[tc] if i2 == 0
                                   else (c.psC_lo_b, c.psC_hi_b)[tc]))
                            prog.op(
                                "tensor",
                                lambda e, pst=pst, o0=o0, tc=tc, soff=soff,
                                hg=hg, last=last, xnT=xnT, spw=spw: e.matmul(
                                    pst[:, o0 + tc * 512:o0 + (tc + 1) * 512],
                                    spw[:, hg, soff:soff + 128],
                                    xnT[:, hg, tc * 512:(tc + 1) * 512],
                                    start=False, stop=last),
                                reads=[sp_bufs[hg], xnT_bufs[hg]],
                                writes=[pb])
            for i2, ip in enumerate(ipg):
                hs_s, hs_sb = hstage[i2]
                upst = c.psB if i2 == 0 else c.psC
                upb = ([c.psB_lo_b, c.psB_hi_b] if i2 == 0
                       else [c.psC_lo_b, c.psC_hi_b])
                sg_t, sg_tb = sg_stage[i2]
                prog.op("scalar", lambda e, i2=i2, sg_t=sg_t: e.activation(
                    out=sg_t[:], in_=c.psA[:, i2 * 1024:(i2 + 1) * 1024],
                    func=AF.Silu),
                    reads=[c.psA_lo_b if i2 == 0 else c.psA_hi_b],
                    writes=[sg_tb])
                prog.op("vector", lambda e, sg_t=sg_t, upst=upst, hs_s=hs_s:
                        e.tensor_tensor(
                            out=hs_s[:], in0=sg_t[:], in1=upst[:, 0:1024],
                            op=ALU.mult),
                        reads=[sg_tb] + upb, writes=[hs_sb])
                prog.op("gpsimd", lambda e, ip=ip, ts=ts, hs_s=hs_s:
                        e.dma_start(out=hsc_d[ip, ts], in_=hs_s[:]),
                        reads=[hs_sb], writes=[hsc_b], dma="hso%d" % i2)

    R.reset_to(chk0)
    hslab, hslab_bufs = R.carve_chunks("hslab", 86, 512, BF16)
    wmov = [R.carve(f"wmov{i}", [128, 1024], BF16) for i in range(4)]
    xr4l = [R.carve(f"xr4{i}", [128, 1024], F32) for i in range(2)]
    x4sl = [R.carve(f"x4s{i}", [128, 1024], F32) for i in range(2)]
    pslots = [(c.psA, [c.psA_lo_b, c.psA_lo_b], 0),
              (c.psA, [c.psA_hi_b, c.psA_hi_b], 1024),
              (c.psB, [c.psB_lo_b, c.psB_hi_b], 0),
              (c.psC, [c.psC_lo_b, c.psC_hi_b], 0)]
    for tsl in range(4):
        ts2, half = tsl // 2, tsl % 2
        for qd in range(4):
            i0 = qd * 22
            i1 = min(i0 + 22, 86)
            src = hsc_d[i0:i1, ts2, :, half * 512:(half + 1) * 512].rearrange(
                "i p a -> p i a")
            prog.op(("gpsimd", "sync")[qd % 2],
                    lambda e, src=src, i0=i0, i1=i1: e.dma_start(
                        out=hslab[:, i0:i1, :], in_=src),
                    reads=[hsc_b],
                    writes=[hslab_bufs[i] for i in range(i0, i1)],
                    dma="hsl%d" % (qd % 2))
        for i in range(86):
            prog.op("tensor", lambda e, i=i: e.matmul(
                c.psA[0:1, 0:512], c.onesn[0][:], hslab[:, i, :],
                start=True, stop=True),
                reads=[c.onesn[1], hslab_bufs[i]], writes=[c.psA_lo_b])
            prog.op("vector", lambda e: e.tensor_copy(
                out=c.hrow[0][:], in_=c.psA[0:1, 0:512]),
                reads=[c.psA_lo_b], writes=[c.hrow[1]])
            prog.op("gpsimd", lambda e, i=i, tsl=tsl: e.dma_start(
                out=hs_d[i:i + 1, tsl * 512:(tsl + 1) * 512],
                in_=c.hrow[0][:]),
                reads=[c.hrow[1]], writes=[hs_b], dma="hss")
        prog.op("gpsimd", lambda e, tsl=tsl: e.dma_start(
            out=hsT[:], in_=hs_d[:, tsl * 512:(tsl + 1) * 512]),
            reads=[hs_b], writes=[hsT_b], dma="hstl")
        for oq in range(4):
            prog.op("gpsimd", lambda e, oq=oq: e.dma_start(
                out=zdn[:], in_=zsd_d[:, oq * 1024:(oq + 1) * 1024]),
                reads=[zsd_b], writes=[zdn_b], dma="zdnl")
            for tl in range(4):
                pst, pbl, o0 = pslots[tl]
                for oc in range(2):
                    prog.op("tensor",
                            lambda e, pst=pst, o0=o0, oc=oc, tl=tl: e.matmul(
                                pst[:, o0 + oc * 512:o0 + (oc + 1) * 512],
                                hsT[:, tl * 128:(tl + 1) * 128],
                                zdn[:, oc * 512:(oc + 1) * 512],
                                start=True, stop=False),
                            reads=[hsT_b, zdn_b], writes=[pbl[oc]])
            for i in range(86):
                wm, wm_b = wmov[i % 4]
                prog.op(("sync", "scalar")[i % 2],
                        lambda e, i=i, oq=oq, wm=wm: e.dma_start(
                    out=wm[:], in_=wdn_d[i, :, oq * 1024:(oq + 1) * 1024]),
                    reads=[wdn_b], writes=[wm_b], dma=f"wdl{i % 6}")
                for tl in range(4):
                    pst, pbl, o0 = pslots[tl]
                    last = (i == 85)
                    for oc in range(2):
                        prog.op("tensor",
                                lambda e, pst=pst, o0=o0, oc=oc, i=i, tl=tl,
                                wm=wm, last=last: e.matmul(
                                    pst[:, o0 + oc * 512:o0 + (oc + 1) * 512],
                                    hslab[:, i, tl * 128:(tl + 1) * 128],
                                    wm[:, oc * 512:(oc + 1) * 512],
                                    start=False, stop=last),
                                reads=[hslab_bufs[i], wm_b],
                                writes=[pbl[oc]])
            for tl in range(4):
                pst, pbl, o0 = pslots[tl]
                tt = tsl * 4 + tl
                xr4, xr4_b = xr4l[tl % 2]
                x4s, x4s_b = x4sl[tl % 2]
                prog.op("gpsimd", lambda e, tt=tt, oq=oq, xr4=xr4:
                        e.dma_start(
                            out=xr4[:], in_=x_d[tt * 128:(tt + 1) * 128,
                                                oq * 1024:(oq + 1) * 1024]),
                        reads=[x_b], writes=[xr4_b], dma="xr%d" % (tl % 2))
                prog.op("vector", lambda e, pst=pst, o0=o0, x4s=x4s, xr4=xr4:
                        e.tensor_tensor(
                    out=x4s[:], in0=pst[:, o0:o0 + 1024], in1=xr4[:],
                    op=ALU.add),
                    reads=pbl + [xr4_b], writes=[x4s_b])
                prog.op("gpsimd", lambda e, tt=tt, oq=oq, x4s=x4s: e.dma_start(
                    out=io["x4"][tt * 128:(tt + 1) * 128,
                                 oq * 1024:(oq + 1) * 1024], in_=x4s[:]),
                    reads=[x4s_b], writes=[io["x4__b"]], dma="x4o")


def build_fused():
    nc = bass.Bass("TRN2", target_bir_lowering=False)
    io = declare_io(nc)
    prog = Prog(nc)
    with ExitStack() as ex:
        c = Ctx(nc, prog, ex)
        c.alloc_psum()
        alloc_globals(c, io)
        R = Region(c, "R", 171)
        emit_l0_attn(c, R, io)
        emit_l0_ffn(c, R, io)
        emit_l1_attn(c, R, io)
        emit_l1_ffn(c, R, io)
        with nc.Block() as block:
            prog.finalize(block)
    return nc


def kernel(L0_attn_q, L0_attn_w_qkv, L0_attn_w_dense, L0_ffn_A_w, L0_ffn_B_w,
           L0_ffn_Wq_w, L0_ffn_keys, L1_attn_wq_a, L1_attn_q_a_ln_w,
           L1_attn_wq_b, L1_attn_wkv_a, L1_attn_kv_a_ln_w, L1_attn_wkv_b,
           L1_attn_wo, L1_ffn_qweight_gu, L1_ffn_qzeros_gu, L1_ffn_scales_gu,
           L1_ffn_qweight_dn, L1_ffn_qzeros_dn, L1_ffn_scales_dn):
    n_cores = 8
    x = np.ascontiguousarray(np.asarray(L0_attn_q, dtype=np.float32))
    wd = host_prep_weights(dict(
        L0_attn_w_qkv=L0_attn_w_qkv, L0_attn_w_dense=L0_attn_w_dense,
        L0_ffn_A_w=L0_ffn_A_w, L0_ffn_B_w=L0_ffn_B_w, L0_ffn_Wq_w=L0_ffn_Wq_w,
        L0_ffn_keys=L0_ffn_keys, L1_attn_wq_a=L1_attn_wq_a,
        L1_attn_q_a_ln_w=L1_attn_q_a_ln_w, L1_attn_wq_b=L1_attn_wq_b,
        L1_attn_wkv_a=L1_attn_wkv_a, L1_attn_kv_a_ln_w=L1_attn_kv_a_ln_w,
        L1_attn_wkv_b=L1_attn_wkv_b, L1_attn_wo=L1_attn_wo,
        L1_ffn_qweight_gu=L1_ffn_qweight_gu, L1_ffn_qzeros_gu=L1_ffn_qzeros_gu,
        L1_ffn_scales_gu=L1_ffn_scales_gu, L1_ffn_qweight_dn=L1_ffn_qweight_dn,
        L1_ffn_qzeros_dn=L1_ffn_qzeros_dn, L1_ffn_scales_dn=L1_ffn_scales_dn))
    nc = build_fused()
    in_maps = [dict(wd, x=x[i]) for i in range(n_cores)]
    res = run_bass_kernel_spmd(nc, in_maps, core_ids=list(range(n_cores)))
    return np.stack([r["x4"] for r in res.results]).astype(np.float32)


# version 23
# speedup vs baseline: 1.0418x; 1.0418x over previous
import math
from contextlib import ExitStack

import numpy as np
import ml_dtypes
import concourse.bass as bass
import concourse.mybir as mybir
from concourse.bass_utils import run_bass_kernel_spmd

F32 = mybir.dt.float32
BF16 = mybir.dt.bfloat16
I32 = mybir.dt.int32
AF = mybir.ActivationFunctionType
ALU = mybir.AluOpType
AX = mybir.AxisListType
BF = ml_dtypes.bfloat16

T = 2048
D = 4096
SLAB = 1024
NEG = -1.0e30

ENGINES = ("sync", "tensor", "vector", "scalar", "gpsimd")


class Buf:
    __slots__ = ("name", "w_tok", "readers")

    def __init__(self, name="b", readers=None):
        self.name = name
        self.w_tok = None
        self.readers = list(readers) if readers else []


class Prog:
    def __init__(self, nc):
        self.nc = nc
        self.counts = {e: 0 for e in ENGINES}
        self.ops = {e: [] for e in ENGINES}
        self.sems = {}
        self.scope = "init"

    def dma_token(self, stream):
        return ("D" + stream, self.counts.get("D" + stream, 0))

    def op(self, eng, fn, reads=(), writes=(), deps=(), dma=None):
        dep_toks = []
        for b in reads:
            if b.w_tok is not None:
                dep_toks.append(b.w_tok)
        for b in writes:
            if b.w_tok is not None:
                dep_toks.append(b.w_tok)
            dep_toks.extend(b.readers)
        dep_toks.extend(deps)
        merged = {}
        for (e2, v) in dep_toks:
            if e2 == eng and v > self.counts[eng]:
                raise RuntimeError("forward self dep")
            if e2 == "tensor" and eng == "tensor":
                continue
            if merged.get(e2, -1) < v:
                merged[e2] = v
        if dma is None:
            tok = (eng, self.counts[eng] + 1)
            self.counts[eng] = tok[1]
        else:
            key = "D" + dma
            tok = (key, self.counts.get(key, 0) + 16)
            self.counts[key] = tok[1]
        self.ops[eng].append((fn, merged, tok, self.scope))
        for b in reads:
            b.readers.append(tok)
        for b in writes:
            b.w_tok = tok
            b.readers = []
        return tok

    def finalize(self, block):
        for key in self.counts:
            self.sems[key] = self.nc.alloc_semaphore(name=f"sem_{key}")

        def make_body(ename):
            def body(eng):
                waited = {}
                cur = None
                sid = None
                for (fn, merged, tok, scope) in self.ops[ename]:
                    if scope != cur:
                        if cur is not None:
                            self.nc.leave_named_scope(cur, sid, False)
                        sid, _ = self.nc.enter_named_scope(scope, False)
                        cur = scope
                    for e2, v in sorted(merged.items()):
                        if waited.get(e2, 0) < v:
                            eng.wait_ge(self.sems[e2], v)
                            waited[e2] = v
                    key, val = tok
                    fn(eng).then_inc(self.sems[key],
                                     16 if key[0] == "D" else 1)
                if cur is not None:
                    self.nc.leave_named_scope(cur, sid, False)
                if ename == "sync":
                    for e2, cnt in self.counts.items():
                        if e2 != "sync" and cnt > waited.get(e2, 0):
                            eng.wait_ge(self.sems[e2], cnt)
            return body

        for e in ENGINES:
            getattr(block, e)(make_body(e))


class Region:

    def __init__(self, ctx, name, kb_per_partition):
        self.n = kb_per_partition * 512
        self.back = ctx.ex.enter_context(
            ctx.nc.sbuf_tensor(name, [128, self.n], BF16))
        self.off = 0
        self.live = []
        self.pending = []

    def carve(self, name, shape, dtype):
        assert shape[0] <= 128
        elems = int(np.prod(shape[1:]))
        width = elems * (2 if dtype in (F32, I32) else 1)
        assert self.off + width <= self.n, (name, self.off, width, self.n)
        v = self.back[:, self.off:self.off + width]
        if dtype != BF16:
            v = v.bitcast(dtype)
        if len(shape) > 2:
            v = v.rearrange("p (a b) -> p a b", a=shape[1])
        if shape[0] < 128:
            v = v[0:shape[0]]
        self.off += width
        b = Buf(name, readers=self.pending)
        self.live.append(b)
        return v, b

    def carve_chunks(self, name, n_chunks, tlen, dtype):
        v, b0 = self.carve(name, [128, n_chunks * tlen], dtype)
        v = v.rearrange("p (a b) -> p a b", a=n_chunks)
        self.live.remove(b0)
        bufs = [Buf(f"{name}_{i}", readers=self.pending)
                for i in range(n_chunks)]
        self.live.extend(bufs)
        return v, bufs

    def checkpoint(self):
        return (self.off, len(self.live), list(self.pending))

    def reset_to(self, chk):
        off, nlive, pend = chk
        toks = list(pend)
        for b in self.live[nlive:]:
            if b.w_tok is not None:
                toks.append(b.w_tok)
            toks.extend(b.readers)
        self.pending = toks
        self.live = self.live[:nlive]
        self.off = off

    def reset(self):
        self.reset_to((0, 0, self.pending))


class Ctx:
    def __init__(self, nc, prog, ex):
        self.nc = nc
        self.p = prog
        self.ex = ex

    def sb(self, name, shape, dtype):
        t = self.ex.enter_context(self.nc.sbuf_tensor(name, shape, dtype))
        return t, Buf(name)

    def alloc_psum(self):
        self.psA = self.ex.enter_context(
            self.nc.psum_tensor("psA", [128, 2048], F32))
        self.psB = self.ex.enter_context(
            self.nc.psum_tensor("psB", [128, 1024], F32))
        self.psC = self.ex.enter_context(
            self.nc.psum_tensor("psC", [128, 1024], F32))
        self.psA_lo_b, self.psA_hi_b = Buf("psAlo"), Buf("psAhi")
        self.psB_lo_b, self.psB_hi_b = Buf("psBlo"), Buf("psBhi")
        self.psC_lo_b, self.psC_hi_b = Buf("psClo"), Buf("psChi")


def to_f32c(a):
    return np.ascontiguousarray(np.asarray(a, dtype=np.float32))


def ident_np():
    return np.eye(128, dtype=BF)


def diagmask_np():
    q = np.arange(128)[:, None]
    k = np.arange(128)[None, :]
    return np.where(k <= q, 0.0, NEG).astype(BF)


def yarn_cos_sin_np(seq_len, dim=64, base=10000.0, factor=128.0,
                    beta_fast=32.0, beta_slow=1.0, orig_max=8192):
    def corr_dim(r):
        return dim * math.log(orig_max / (r * 2 * math.pi)) / (2 * math.log(base))
    low = max(math.floor(corr_dim(beta_fast)), 0)
    high = min(math.ceil(corr_dim(beta_slow)), dim - 1)
    hi = high + 0.001 if low == high else float(high)
    ramp = np.clip((np.arange(dim // 2, dtype=np.float32) - low) / (hi - low),
                   0.0, 1.0)
    inv_freq_mask = 1.0 - ramp
    freq_extra = 1.0 / base ** (np.arange(0, dim, 2, dtype=np.float32) / dim)
    freq_inter = freq_extra / factor
    inv_freq = freq_inter * (1.0 - inv_freq_mask) + freq_extra * inv_freq_mask
    t = np.arange(seq_len, dtype=np.float32)
    freqs = np.outer(t, inv_freq)
    emb = np.concatenate([freqs, freqs], axis=-1)
    return np.cos(emb).astype(np.float32), np.sin(emb).astype(np.float32)


def deint_perm(n=64):
    p = np.empty(n, np.int64)
    half = n // 2
    p[:half] = np.arange(half) * 2
    p[half:] = np.arange(half) * 2 + 1
    return p


def host_prep_weights(w):
    out = {}
    out["wqkvT"] = to_f32c(np.asarray(w["L0_attn_w_qkv"]).T)
    out["wdenseT"] = to_f32c(np.asarray(w["L0_attn_w_dense"]).T)
    perm = np.arange(512).reshape(64, 8).T.reshape(-1)
    A_p = np.asarray(w["L0_ffn_A_w"])[perm]
    B_p = np.asarray(w["L0_ffn_B_w"])[:, perm]
    out["A_wT"] = to_f32c(A_p.T)
    out["B_wT"] = to_f32c(B_p.T)
    out["Wq_wT"] = to_f32c(np.asarray(w["L0_ffn_Wq_w"]).T)
    out["keysT"] = to_f32c(np.asarray(w["L0_ffn_keys"]).T)
    wq_b = np.asarray(w["L1_attn_wq_b"])
    pm = deint_perm()
    wq_b_p = wq_b.copy()
    for h in range(32):
        base = h * 128 + 64
        wq_b_p[base:base + 64] = wq_b[base + pm]
    wkv_a = np.asarray(w["L1_attn_wkv_a"])
    wkv_a_p = wkv_a.copy()
    wkv_a_p[256:320] = wkv_a[256 + pm]
    wkv_b3 = np.asarray(w["L1_attn_wkv_b"]).reshape(32, 192, 256)
    out["wq_aT"] = to_f32c(np.asarray(w["L1_attn_wq_a"]).T)
    out["q_ln"] = to_f32c(np.asarray(w["L1_attn_q_a_ln_w"]).reshape(1, 1024))
    out["wq_bT"] = to_f32c(wq_b_p.T)
    out["wkv_aT"] = to_f32c(wkv_a_p.T)
    out["kv_ln"] = to_f32c(np.asarray(w["L1_attn_kv_a_ln_w"]).reshape(1, 256))
    out["wkv_b_kT"] = to_f32c(wkv_b3[:, 0:64, :].reshape(2048, 256).T)
    out["wkv_b_vT"] = to_f32c(wkv_b3[:, 64:192, :].reshape(4096, 256).T)
    out["woT"] = to_f32c(np.asarray(w["L1_attn_wo"]).T)
    cos, sin = yarn_cos_sin_np(T)
    sin_eff = sin.copy()
    sin_eff[:, 0:32] *= -1.0
    out["cosT"] = np.ascontiguousarray(cos.T.astype(BF))
    out["sinT"] = np.ascontiguousarray(sin_eff.T.astype(BF))
    out["qw_gu"] = np.ascontiguousarray(np.asarray(w["L1_ffn_qweight_gu"]))
    out["qz_gu"] = np.ascontiguousarray(np.asarray(w["L1_ffn_qzeros_gu"]))
    out["s_gu"] = to_f32c(w["L1_ffn_scales_gu"])
    out["qw_dn"] = np.ascontiguousarray(np.asarray(w["L1_ffn_qweight_dn"]))
    out["qz_dn"] = np.ascontiguousarray(np.asarray(w["L1_ffn_qzeros_dn"]))
    out["qz_dn"] = np.ascontiguousarray(np.asarray(w["L1_ffn_qzeros_dn"]))
    out["s_dn"] = to_f32c(w["L1_ffn_scales_dn"])
    out["ident"] = ident_np()
    out["diagmask"] = diagmask_np()
    return out


INPUT_SPECS = {
    "x": ([T, D], F32),
    "wqkvT": ([D, 6144], F32),
    "wdenseT": ([D, D], F32),
    "A_wT": ([D, 512], F32),
    "B_wT": ([512, D], F32),
    "Wq_wT": ([D, 16], F32),
    "keysT": ([16, 64], F32),
    "wq_aT": ([D, 1024], F32),
    "q_ln": ([1, 1024], F32),
    "wq_bT": ([1024, D], F32),
    "wkv_aT": ([D, 320], F32),
    "kv_ln": ([1, 256], F32),
    "wkv_b_kT": ([256, 2048], F32),
    "wkv_b_vT": ([256, D], F32),
    "woT": ([D, D], F32),
    "cosT": ([64, T], BF16),
    "sinT": ([64, T], BF16),
    "qw_gu": ([D, 2752], I32),
    "qz_gu": ([32, 2752], I32),
    "s_gu": ([32, 22016], F32),
    "qw_dn": ([11008, 512], I32),
    "qz_dn": ([86, 512], I32),
    "s_dn": ([86, 4096], F32),
    "ident": ([128, 128], BF16),
    "diagmask": ([128, 128], BF16),
}

SCRATCH_SPECS = {
    "hv_sc": ([D, T], BF16),
    "q_sc": ([32, 128, T], BF16),
    "v_sc": ([32, 128, T], BF16),
    "pj_sc": ([32, 16, 128, 128], BF16),
    "x1_sc": ([T, D], F32),
    "x2_sc": ([T, D], F32),
    "x3_sc": ([T, D], F32),
    "wgu_deq": ([32, 128, 22016], BF16),
    "wdn_deq": ([86, 128, 4096], BF16),
    "shl_gu": ([2, 32, 22016], BF16),
    "shl_dn": ([2, 86, 4096], BF16),
    "zs_gu": ([32, 22016], BF16),
    "zs_dn": ([86, 4096], BF16),
    "xns": ([32, T], BF16),
    "hs": ([86, T], BF16),
    "h_sc": ([86, 2, 128, 1024], BF16),
}


def declare_io(nc, fused=True):
    io = {}
    for name, (shape, dt) in INPUT_SPECS.items():
        io[name] = nc.dram_tensor(name, shape, dt, kind="ExternalInput").ap()
        io[name + "__b"] = Buf(name)
    for name, (shape, dt) in SCRATCH_SPECS.items():
        io[name] = nc.dram_tensor(name, shape, dt, kind="Internal").ap()
        io[name + "__b"] = Buf(name)
    io["x4"] = nc.dram_tensor("x4", [T, D], F32, kind="ExternalOutput").ap()
    io["x4__b"] = Buf("x4")
    return io


def alloc_globals(c, io):
    p = c.p
    c.ident, c.ident_b = c.sb("identsb", [128, 128], BF16)
    c.mask, c.mask_b = c.sb("diagmasksb", [128, 128], BF16)
    p.op("sync", lambda e: e.dma_start(out=c.ident[:], in_=io["ident"]),
         writes=[c.ident_b], dma="cst")
    p.op("sync", lambda e: e.dma_start(out=c.mask[:], in_=io["diagmask"]),
         writes=[c.mask_b], dma="cst")
    tok = p.dma_token("cst")
    c.ident_b.w_tok = tok
    c.mask_b.w_tok = tok
    c.lnx = c.sb("lnx", [128, D], F32)
    c.lnn = c.sb("lnn", [128, D], BF16)
    c.lnstats = c.sb("lnstats", [128, 48], F32)
    c.lnmv = c.sb("lnmv", [128, 2], F32)
    c.lnr = c.sb("lnr", [128, 1], F32)
    c.epsc = c.sb("epsc", [128, 1], F32)
    c.eps6 = c.sb("eps6", [128, 1], F32)
    p.op("vector", lambda e: e.memset(c.epsc[0][:], 1e-5), writes=[c.epsc[1]])
    p.op("vector", lambda e: e.memset(c.eps6[0][:], 1e-6), writes=[c.eps6[1]])
    c.onesn = c.sb("onesn", [128, 1], BF16)
    p.op("vector", lambda e: e.memset(c.onesn[0][:], -1.0), writes=[c.onesn[1]])
    c.rowm = c.sb("rowm", [128, 1], F32)
    c.rowm2 = c.sb("rowm2", [128, 1], F32)
    c.negm = c.sb("negm", [128, 4], F32)
    c.negm_bl = [Buf(f"negmc{i}") for i in range(4)]
    c.denom = c.sb("denom", [128, 4], F32)
    c.denom2 = c.sb("denom2", [128, 8], F32)
    c.rden = c.sb("rden", [128, 1], F32)
    c.diag = c.sb("diag", [128, 1024], BF16)
    c.diag_bl = [Buf("diagp0"), Buf("diagp1")]
    c.msq = c.sb("msq", [128, 1], F32)
    c.rrms = c.sb("rrms", [128, 1], F32)
    c.hvq = c.sb("hvq", [128, 512], BF16)
    c.xrow = c.sb("xrow", [1, 1024], BF16)
    c.hrow = c.sb("hrow", [1, 512], BF16)
    c.wst_wf = c.sb("wst_wf", [128, 256], F32)
    c.wst_w0 = c.sb("wst_w0", [128, 128], BF16)


def emit_ln_slab(c, x_d, x_b, s, xT, xT_bufs, eps=1e-5):
    p = c.p
    psB_bf = c.psB[:].bitcast(BF16)
    for tl in range(SLAB // 128):
        tt = s * 8 + tl
        xt, xt_b = c.lnx
        xnt, xnt_b = c.lnn
        p.op("gpsimd", lambda e, tt=tt, xt=xt: e.dma_start(
            out=xt[:], in_=x_d[tt * 128:(tt + 1) * 128, :]),
            reads=[x_b], writes=[xt_b], dma="xl")
        for ch in range(8):
            p.op("vector", lambda e, ch=ch, xt=xt: e.bn_stats(
                out=c.lnstats[0][:, ch * 6:(ch + 1) * 6],
                in_=xt[:, ch * 512:(ch + 1) * 512]),
                reads=[xt_b], writes=[c.lnstats[1]])
        p.op("vector", lambda e: e.bn_aggr(out=c.lnmv[0][:], in_=c.lnstats[0][:]),
             reads=[c.lnstats[1]], writes=[c.lnmv[1]])
        p.op("scalar", lambda e: e.activation(
            out=c.lnr[0][:], in_=c.lnmv[0][:, 1:2], func=AF.Ln,
            bias=c.epsc[0][:]),
            reads=[c.lnmv[1], c.epsc[1]], writes=[c.lnr[1]])
        p.op("scalar", lambda e: e.activation(
            out=c.lnr[0][:], in_=c.lnr[0][:], func=AF.Exp, scale=-0.5),
            reads=[c.lnr[1]], writes=[c.lnr[1]])
        p.op("vector", lambda e, xt=xt, xnt=xnt: e.tensor_scalar(
            out=xnt[:], in0=xt[:], scalar1=c.lnmv[0][:, 0:1],
            scalar2=c.lnr[0][:], op0=ALU.subtract, op1=ALU.mult),
            reads=[xt_b, c.lnmv[1], c.lnr[1]], writes=[xnt_b])
        for g in range(4):
            for j in range(g * 8, g * 8 + 8):
                p.op("tensor", lambda e, j=j, g=g, xnt=xnt: e.transpose(
                    psB_bf[:, (j - g * 8) * 128:(j - g * 8 + 1) * 128],
                    xnt[:, j * 128:(j + 1) * 128], c.ident[:]),
                    reads=[xnt_b, c.ident_b], writes=[c.psB_lo_b])
            src = psB_bf[:, 0:1024].rearrange("p (j t) -> p j t", j=8)
            dst = xT[:, g * 8:g * 8 + 8, tl * 128:(tl + 1) * 128]
            p.op("vector", lambda e, src=src, dst=dst: e.tensor_copy(
                out=dst, in_=src),
                reads=[c.psB_lo_b],
                writes=[xT_bufs[j] for j in range(g * 8, g * 8 + 8)])


def emit_wproj(c, w_d, w_b, actT, actT_bufs, act_tlen, in_chunks, out_tiles,
               evac_cb, engs=("sync", "scalar")):
    assert act_tlen <= 1024
    p = c.p
    if not hasattr(c, "wring"):
        c.wring = [c.sb(f"wring{i}", [128, 256], F32) for i in range(3)]
        c.wringb = [c.sb(f"wringb{i}", [128, 256], BF16) for i in range(3)]
    n_tc = (act_tlen + 511) // 512
    groups = [out_tiles[i:i + 2] for i in range(0, len(out_tiles), 2)]
    ric = 0
    if not hasattr(c, "wproj_pp"):
        c.wproj_pp = 0
    for gi, grp in enumerate(groups):
        pool = c.wproj_pp % 2
        c.wproj_pp += 1
        if pool == 0:
            slots = [(c.psA[:, 0:1024], [c.psA_lo_b]),
                     (c.psA[:, 1024:2048], [c.psA_hi_b])]
        else:
            slots = [(c.psB[:, 0:1024], [c.psB_lo_b, c.psB_hi_b]),
                     (c.psC[:, 0:1024], [c.psC_lo_b, c.psC_hi_b])]
        nw = len(grp) * 128
        for ki, ic in enumerate(in_chunks):
            slot = ric % 3
            ric += 1
            wf, wf_b = c.wring[slot]
            wc, wc_b = c.wringb[slot]
            q = "wsA" if ki % 2 == 0 else "wsB"
            eng = engs[ki % 2]
            p.op(eng, lambda e, ic=ic, g0=grp[0], nw=nw, wf=wf: e.dma_start(
                out=wf[:, 0:nw],
                in_=w_d[ic * 128:(ic + 1) * 128, g0 * 128:g0 * 128 + nw]),
                reads=[w_b], writes=[wf_b], dma=q + str(slot))
            p.op("vector", lambda e, wf=wf, wc=wc, nw=nw: e.tensor_copy(
                out=wc[:, 0:nw], in_=wf[:, 0:nw]),
                reads=[wf_b], writes=[wc_b])
            first = (ki == 0)
            last = (ki == len(in_chunks) - 1)
            for j in range(len(grp)):
                ps, ps_bl = slots[j]
                for tc in range(n_tc):
                    t0, t1 = tc * 512, min((tc + 1) * 512, act_tlen)
                    mov = actT[:, ic, t0:t1]
                    p.op("tensor",
                         lambda e, wc=wc, j=j, mov=mov, ps=ps, t0=t0, t1=t1,
                         first=first, last=last: e.matmul(
                             ps[:, t0:t1], wc[:, j * 128:(j + 1) * 128], mov,
                             start=first, stop=last),
                         reads=[wc_b, actT_bufs[ic]],
                         writes=[ps_bl[(t0 // 512) % len(ps_bl)]])
        for j, ft in enumerate(grp):
            ps, ps_bl = slots[j]
            evac_cb(gi * 2 + j, ps, ps_bl)


def emit_attn_heads(c, R, kT_get, v_get, q_get, scale, hv_dram, hv_b,
                    pbufs, ptb, ptb_bufs, qch_list, q_pair_get=None):
    prog = c.p
    diag2 = c.diag[0].rearrange("p (u i a) -> p u i a", u=2, i=4)
    qpair = None
    cnt = [0]
    pend = [None]

    def flush():
        if pend[0] is None:
            return
        (qh0, qcg, par, v_fn, v_reads, pb_v, pb_b) = pend[0]
        pend[0] = None
        dg_b = c.diag_bl[par]
        n_kt = (qcg + 1) * 4

        def emit_pv(km, last):
            im2 = max(0, km - qcg * 4)
            vsl = v_fn(km)
            prog.op("tensor",
                    lambda e, km=km, im2=im2, vsl=vsl, last=last: e.matmul(
                        c.psC[:, im2 * 128:512], vsl,
                        ptb[:, km % 3, im2 * 128:512],
                        start=(km == 0), stop=last),
                    reads=v_reads + [ptb_bufs[km % 3]], writes=[c.psC_lo_b])
        for kt in range(n_kt):
            i_min = max(0, kt - qcg * 4)
            if kt >= 1:
                emit_pv(kt - 1, False)
            for i in range(i_min, 4):
                prog.op("tensor",
                        lambda e, kt=kt, i=i, pb_v=pb_v, par=par: e.matmul(
                            c.psB[:, i * 128:(i + 1) * 128],
                            pb_v[:, i * T + kt * 128:i * T + (kt + 1) * 128],
                            diag2[:, par, i, :], start=True, stop=True),
                        reads=[pb_b, dg_b], writes=[c.psB_lo_b])
            prog.op("vector",
                    lambda e, kt=kt, i_min=i_min: e.tensor_copy(
                out=ptb[:, kt % 3, i_min * 128:512],
                in_=c.psB[:, i_min * 128:512]),
                reads=[c.psB_lo_b], writes=[ptb_bufs[kt % 3]])
        emit_pv(n_kt - 1, True)
        prog.op("vector", lambda e: e.tensor_copy(
            out=c.hvq[0][:], in_=c.psC[:, 0:512]),
            reads=[c.psC_lo_b], writes=[c.hvq[1]])
        prog.op("gpsimd", lambda e, qh0=qh0, qcg=qcg: e.dma_start(
            out=hv_dram[qh0 * 128:(qh0 + 1) * 128,
                        qcg * 512:(qcg + 1) * 512], in_=c.hvq[0][:]),
            reads=[c.hvq[1]], writes=[hv_b], dma="hvs")

    for qh in range(32):
        if q_pair_get is not None:
            if qh % 2 == 0:
                flush()
                qpair = q_pair_get(qh // 2)
            q_ap, q_b = qpair[qh % 2]
        else:
            flush()
            q_ap, q_b = q_get(qh)
        kT_ap, kT_b = kT_get(qh)
        v_fn, v_reads = v_get(qh)
        for qcg in qch_list:
            par = cnt[0] % 2
            cnt[0] += 1
            pb_v, pb_b = pbufs[par]
            tiles = []
            for i in range(4):
                qt = qcg * 4 + i
                k_len = (qt + 1) * 128
                ql = qt * 128 - (qch_list[0] * 512)
                a_off = 0 if i % 2 == 0 else 1024
                a_buf = c.psA_lo_b if i % 2 == 0 else c.psA_hi_b
                segs = [(c.psA, a_off, min(k_len, 1024), a_buf, 0)]
                rem = k_len - 1024
                if 0 < rem <= 512:
                    if i % 2 == 0:
                        segs.append((c.psB, 512, rem, c.psB_hi_b, 1024))
                    else:
                        segs.append((c.psC, 512, rem, c.psC_hi_b, 1024))
                elif rem > 512:
                    segs.append((c.psB, 512, 512, c.psB_hi_b, 1024))
                    segs.append((c.psC, 512, rem - 512, c.psC_hi_b, 1536))
                tiles.append((i, segs))
                for (ps, off, w, buf, ksrc) in segs:
                    for km in range((w + 511) // 512):
                        k0, k1 = km * 512, min((km + 1) * 512, w)
                        prog.op("tensor",
                                lambda e, ps=ps, off=off, k0=k0, k1=k1, ql=ql,
                                ksrc=ksrc, q_ap=q_ap, kT_ap=kT_ap: e.matmul(
                                    ps[:, off + k0:off + k1],
                                    q_ap[:, ql:ql + 128],
                                    kT_ap[:, ksrc + k0:ksrc + k1],
                                    start=True, stop=True),
                                reads=[q_b, kT_b], writes=[buf])
                mps, moff, mw, mbuf, _ = segs[-1]
                mseg = mps[:, moff + mw - 128:moff + mw]
                prog.op("vector", lambda e, mseg=mseg: e.tensor_tensor(
                    out=mseg, in0=mseg, in1=c.mask[:], op=ALU.add),
                    reads=[c.mask_b, mbuf], writes=[mbuf])
                for si, (ps, off, w, buf, ksrc) in enumerate(segs):
                    if si == 0:
                        prog.op("vector",
                                lambda e, ps=ps, off=off, w=w: e.reduce_max(
                                    out=c.rowm[0][:], in_=ps[:, off:off + w],
                                    axis=AX.X),
                                reads=[buf], writes=[c.rowm[1]])
                    else:
                        prog.op("vector",
                                lambda e, ps=ps, off=off, w=w: e.reduce_max(
                                    out=c.rowm2[0][:], in_=ps[:, off:off + w],
                                    axis=AX.X),
                                reads=[buf], writes=[c.rowm2[1]])
                        prog.op("vector", lambda e: e.tensor_tensor(
                            out=c.rowm[0][:], in0=c.rowm[0][:],
                            in1=c.rowm2[0][:], op=ALU.max),
                            reads=[c.rowm[1], c.rowm2[1]], writes=[c.rowm[1]])
                prog.op("vector", lambda e, i=i: e.tensor_scalar_mul(
                    out=c.negm[0][:, i:i + 1], in0=c.rowm[0][:],
                    scalar1=-scale),
                    reads=[c.rowm[1]], writes=[c.negm_bl[i]])
                for si, (ps, off, w, buf, ksrc) in enumerate(segs):
                    dslot = (c.denom[0][:, i:i + 1] if si == 0 else
                             c.denom2[0][:, (si - 1) * 4 + i:
                                          (si - 1) * 4 + i + 1])
                    dbuf = c.denom[1] if si == 0 else c.denom2[1]
                    prog.op("scalar",
                            lambda e, ps=ps, off=off, w=w, i=i, ksrc=ksrc,
                            pb_v=pb_v, dslot=dslot: e.activation(
                                out=pb_v[:, i * T + ksrc:i * T + ksrc + w],
                                in_=ps[:, off:off + w],
                                func=AF.Exp, scale=scale,
                                bias=c.negm[0][:, i:i + 1],
                                accum_out=dslot),
                            reads=[buf, c.negm_bl[i]],
                            writes=[pb_b, dbuf])
            for (i, segs) in tiles:
                for si in range(1, len(segs)):
                    prog.op("vector", lambda e, i=i, si=si: e.tensor_tensor(
                        out=c.denom[0][:, i:i + 1],
                        in0=c.denom[0][:, i:i + 1],
                        in1=c.denom2[0][:, (si - 1) * 4 + i:
                                         (si - 1) * 4 + i + 1], op=ALU.add),
                        reads=[c.denom[1], c.denom2[1]],
                        writes=[c.denom[1]])
                prog.op("vector", lambda e, i=i: e.reciprocal(
                    out=c.rden[0][:], in_=c.denom[0][:, i:i + 1]),
                    reads=[c.denom[1]], writes=[c.rden[1]])
                prog.op("vector", lambda e, i=i, par=par: e.tensor_scalar_mul(
                    out=diag2[:, par, i, :], in0=c.ident[:],
                    scalar1=c.rden[0][:]),
                    reads=[c.ident_b, c.rden[1]], writes=[c.diag_bl[par]])
            flush()
            pend[0] = (qh, qcg, par, v_fn, v_reads, pb_v, pb_b)
    flush()


def emit_wo_residual(c, R, io, w_key, x_ap, x_b, out_ap, out_b):
    prog = c.p
    hv_dram, hv_b = io["hv_sc"], io["hv_sc__b"]
    w_d, w_b = io[w_key], io[w_key + "__b"]
    psA_bf = c.psA[:].bitcast(BF16)
    chk = R.checkpoint()
    x1s, x1s_b = c.lnx
    for h2 in range(2):
        R.reset_to(chk)
        hvT, hvT_bufs = R.carve_chunks(f"hvT{h2}", 32, SLAB, BF16)
        projT, projT_bufs = R.carve_chunks(f"projT{h2}", 32, SLAB, BF16)
        xrl = [R.carve(f"xrD{h2}{i}", [128, D], F32) for i in range(2)]
        for fch in range(32):
            dst = hvT[:, fch, :]
            prog.op("sync", lambda e, fch=fch, h2=h2, dst=dst: e.dma_start(
                out=dst, in_=hv_dram[fch * 128:(fch + 1) * 128,
                                     h2 * SLAB:(h2 + 1) * SLAB]),
                reads=[hv_b], writes=[hvT_bufs[fch]], dma="hvl")
        tok = prog.dma_token("hvl")
        for b in hvT_bufs:
            b.w_tok = tok

        def oev(j, ps, ps_b, projT=projT, projT_bufs=projT_bufs):
            prog.op("vector", lambda e, j=j, ps=ps, projT=projT:
                    e.tensor_copy(
                out=projT[:, j, :], in_=ps[:, 0:SLAB]),
                reads=list(ps_b), writes=[projT_bufs[j]])
        emit_wproj(c, w_d, w_b, hvT, hvT_bufs, SLAB,
                   list(range(32)), list(range(32)), oev)
        for tl in range(8):
            tt = h2 * 8 + tl
            xr, xr_b = xrl[tl % 2]
            prog.op("gpsimd", lambda e, tt=tt, xr=xr: e.dma_start(
                out=xr[:], in_=x_ap[tt * 128:(tt + 1) * 128, :]),
                reads=[x_b], writes=[xr_b], dma="xr%d" % (tl % 2))
            for och in range(32):
                prog.op("tensor",
                        lambda e, och=och, tl=tl, projT=projT: e.transpose(
                    psA_bf[:, och * 128:(och + 1) * 128],
                    projT[:, och, tl * 128:(tl + 1) * 128],
                    c.ident[:]),
                    reads=[projT_bufs[och], c.ident_b],
                    writes=[c.psA_lo_b if och < 16 else c.psA_hi_b])
            prog.op("vector", lambda e, xr=xr: e.tensor_tensor(
                out=x1s[:], in0=psA_bf[:, 0:D], in1=xr[:], op=ALU.add),
                reads=[c.psA_lo_b, c.psA_hi_b, xr_b], writes=[x1s_b])
            prog.op("gpsimd", lambda e, tt=tt: e.dma_start(
                out=out_ap[tt * 128:(tt + 1) * 128, :], in_=x1s[:]),
                reads=[x1s_b], writes=[out_b], dma="x1o")


def emit_l0_attn(c, R, io):
    prog = c.p
    prog.scope = "s1.pre"
    x_d, x_b = io["x"], io["x__b"]
    psB_bf = c.psB[:].bitcast(BF16)
    R.reset()
    kT, kT_bufs = R.carve_chunks("kT", 8, T, BF16)
    vnat, vnat_bufs = R.carve_chunks("vnat", 16, 1024, BF16)
    vstg = [R.carve("vstg0", [128, SLAB], BF16)]
    qhl = [R.carve(f"qTh{i}", [128, SLAB], BF16) for i in range(2)]
    pbv, pb_bl = R.carve_chunks("pbuf", 2, 4 * T, BF16)
    pbufs = [(pbv[:, 0, :], pb_bl[0]), (pbv[:, 1, :], pb_bl[1])]
    ptb, ptb_bufs = R.carve_chunks("ptbuf", 3, 512, BF16)
    chk = R.checkpoint()
    scale = 1.0 / math.sqrt(128.0)

    for s2 in range(2):
        R.reset_to(chk)
        xnT, xnT_bufs = R.carve_chunks(f"s1xnT{s2}", 32, SLAB, BF16)
        emit_ln_slab(c, x_d, x_b, s2, xnT, xnT_bufs)
        prog.scope = "s1.kvproj"

        def kev(j, ps, ps_b, s2=s2):
            dst = kT[:, j, s2 * SLAB:(s2 + 1) * SLAB]
            prog.op("vector", lambda e, dst=dst: e.tensor_copy(
                out=dst, in_=ps[:, 0:SLAB]),
                reads=list(ps_b), writes=[kT_bufs[j]])
        emit_wproj(c, io["wqkvT"], io["wqkvT__b"], xnT, xnT_bufs, SLAB,
                   list(range(32)), list(range(32, 40)), kev)

        def vev(j, ps, ps_b, s2=s2):
            vt, vt_b = vstg[0]
            prog.op("vector", lambda e, vt=vt: e.tensor_copy(
                out=vt[:], in_=ps[:, 0:SLAB]), reads=list(ps_b), writes=[vt_b])
            for tl in range(8):
                prog.op("tensor", lambda e, vt=vt, tl=tl: e.transpose(
                    psB_bf[:, tl * 128:(tl + 1) * 128],
                    vt[:, tl * 128:(tl + 1) * 128], c.ident[:]),
                    reads=[vt_b, c.ident_b], writes=[c.psB_lo_b])
            src = psB_bf[:, 0:1024].rearrange("p (a t) -> p a t", a=8)
            dst = vnat[:, s2 * 8:s2 * 8 + 8, j * 128:(j + 1) * 128]
            prog.op("vector", lambda e, src=src, dst=dst: e.tensor_copy(
                out=dst, in_=src),
                reads=[c.psB_lo_b],
                writes=[vnat_bufs[s2 * 8 + t] for t in range(8)])
        emit_wproj(c, io["wqkvT"], io["wqkvT__b"], xnT, xnT_bufs, SLAB,
                   list(range(32)), list(range(40, 48)), vev)

        def q_pair_get(qp, s2=s2, xnT=xnT, xnT_bufs=xnT_bufs):
            def qev(j, ps, ps_b):
                qt_s, qt_sb = qhl[j]
                prog.op("vector", lambda e, qt_s=qt_s, qt_sb=qt_sb, ps=ps,
                        ps_b=ps_b: e.tensor_copy(
                    out=qt_s[:], in_=ps[:, 0:SLAB]),
                    reads=list(ps_b), writes=[qt_sb])
            emit_wproj(c, io["wqkvT"], io["wqkvT__b"], xnT, xnT_bufs, SLAB,
                       list(range(32)), [qp * 2, qp * 2 + 1], qev)
            return [qhl[0], qhl[1]]

        def kT_get(qh):
            return kT[:, qh // 4, :], kT_bufs[qh // 4]

        def v_get(qh):
            kvh = qh // 4
            return (lambda kt: vnat[:, kt, kvh * 128:(kvh + 1) * 128],
                    [vnat_bufs[k] for k in range(16)])
        prog.scope = "s1.attn"
        emit_attn_heads(c, R, kT_get, v_get, None, scale,
                        io["hv_sc"], io["hv_sc__b"],
                        pbufs, ptb, ptb_bufs, [s2 * 2, s2 * 2 + 1],
                        q_pair_get=q_pair_get)

    R.reset()
    prog.scope = "s1.wo"
    emit_wo_residual(c, R, io, "wdenseT", x_d, x_b,
                     io["x1_sc"], io["x1_sc__b"])


def emit_l0_ffn(c, R, io):
    prog = c.p
    prog.scope = "s2"
    x_d, x_b = io["x1_sc"], io["x1_sc__b"]
    out_d, out_b = io["x2_sc"], io["x2_sc__b"]
    psB_bf = c.psB[:].bitcast(BF16)
    R.reset()
    xnT, xnT_bufs = R.carve_chunks("s2xnT", 32, T, BF16)
    zg, zg_bufs = R.carve_chunks("zg", 4, T, BF16)
    bwb, bwb_bufs = R.carve_chunks("bwb", 4, 1024, BF16)
    gateT, gateT_b = R.carve("gateT", [64, T], BF16)
    qrb, qrb_b = R.carve("qrb", [128, T], BF16)
    wqb, wqb_b = R.carve("wqb2", [128, 512], BF16)
    keysb, keysb_b = R.carve("keysb", [16, 64], BF16)
    scf, scf_b = R.carve("scf", [128, 64], F32)
    m8a, m8a_b = R.carve("m8a", [128, 8], F32)
    m8b, m8b_b = R.carve("m8b", [128, 8], F32)
    rep, rep_b = R.carve("rep", [128, 64], F32)
    negm1, negm1_b = R.carve("negm1", [128, 1], F32)
    pexp, pexp_b = R.carve("pexp", [128, 64], F32)
    msk, msk_b = R.carve("msk", [128, 64], F32)
    zsum, zsum_b = R.carve("zsum", [128, 1], F32)
    rz, rz_b = R.carve("rz", [128, 1], F32)
    gat, gat_b = R.carve("gat", [128, 64], BF16)
    x2s, x2s_b = R.carve("bx2", [128, 2048], F32)

    for s2 in range(2):
        emit_ln_slab(c, x_d, x_b, s2,
                     xnT[:, :, s2 * SLAB:(s2 + 1) * SLAB], xnT_bufs)
    wf, wf_b = c.wst_wf
    for hlf in range(2):
        prog.op("sync", lambda e, hlf=hlf: e.dma_start(
            out=wf[:].rearrange("p (c f) -> p c f", c=16),
            in_=io["Wq_wT"].rearrange("(a c p) f -> a p c f", a=2, p=128)[hlf]),
            reads=[io["Wq_wT__b"]], writes=[wf_b], dma="wst0")
        prog.op("vector", lambda e, hlf=hlf: e.tensor_copy(
            out=wqb[:, hlf * 256:(hlf + 1) * 256], in_=wf[:]),
            reads=[wf_b], writes=[wqb_b])
    prog.op("sync", lambda e: e.dma_start(out=wf[0:16, 0:64], in_=io["keysT"]),
            reads=[io["keysT__b"]], writes=[wf_b], dma="wst0")
    prog.op("vector", lambda e: e.tensor_copy(out=keysb[:], in_=wf[0:16, 0:64]),
            reads=[wf_b], writes=[keysb_b])
    for e_ in range(32):
        for tc in range(4):
            prog.op("tensor", lambda e, e_=e_, tc=tc: e.matmul(
                c.psA[0:16, tc * 512:(tc + 1) * 512],
                wqb[:, e_ * 16:(e_ + 1) * 16],
                xnT[:, e_, tc * 512:(tc + 1) * 512],
                start=(e_ == 0), stop=(e_ == 31)),
                reads=[wqb_b, xnT_bufs[e_]],
                writes=[c.psA_lo_b if tc < 2 else c.psA_hi_b])
    prog.op("vector", lambda e: e.tensor_copy(
        out=qrb[0:16, :], in_=c.psA[0:16, :]),
        reads=[c.psA_lo_b, c.psA_hi_b], writes=[qrb_b])
    for tt in range(16):
        prog.op("tensor", lambda e, tt=tt: e.matmul(
            c.psC[:, 0:64], qrb[0:16, tt * 128:(tt + 1) * 128],
            keysb[:], start=True, stop=True),
            reads=[qrb_b, keysb_b], writes=[c.psC_lo_b])
        prog.op("vector", lambda e: e.tensor_copy(
            out=scf[:], in_=c.psC[:, 0:64]),
            reads=[c.psC_lo_b], writes=[scf_b])
        prog.op("vector", lambda e: e.max(out=m8a[:], in_=scf[:]),
                reads=[scf_b], writes=[m8a_b])
        prog.op("vector", lambda e: e.match_replace(
            out=rep[:], in_to_replace=m8a[:], in_values=scf[:],
            imm_value=NEG), reads=[scf_b, m8a_b], writes=[rep_b])
        prog.op("vector", lambda e: e.max(out=m8b[:], in_=rep[:]),
                reads=[rep_b], writes=[m8b_b])
        prog.op("vector", lambda e: e.tensor_scalar_mul(
            out=negm1[:], in0=m8a[:, 0:1], scalar1=-1.0),
            reads=[m8a_b], writes=[negm1_b])
        prog.op("scalar", lambda e: e.activation(
            out=pexp[:], in_=scf[:], func=AF.Exp, bias=negm1[:]),
            reads=[scf_b, negm1_b], writes=[pexp_b])
        prog.op("vector", lambda e: e.tensor_single_scalar(
            out=msk[:], in_=scf[:], scalar=m8b[:, 7:8], op=ALU.is_ge),
            reads=[scf_b, m8b_b], writes=[msk_b])
        prog.op("vector", lambda e: e.tensor_mul(
            out=pexp[:], in0=pexp[:], in1=msk[:]),
            reads=[pexp_b, msk_b], writes=[pexp_b])
        prog.op("vector", lambda e: e.reduce_sum(
            out=zsum[:], in_=pexp[:], axis=AX.X),
            reads=[pexp_b], writes=[zsum_b])
        prog.op("vector", lambda e: e.reciprocal(out=rz[:], in_=zsum[:]),
                reads=[zsum_b], writes=[rz_b])
        prog.op("vector", lambda e: e.tensor_scalar(
            out=gat[:], in0=pexp[:], scalar1=rz[:], scalar2=2.0,
            op0=ALU.mult, op1=ALU.mult),
            reads=[pexp_b, rz_b], writes=[gat_b])
        prog.op("tensor", lambda e: e.transpose(
            psB_bf[0:64, 0:128], gat[:], c.ident[:]),
            reads=[gat_b, c.ident_b], writes=[c.psB_lo_b])
        prog.op("vector", lambda e, tt=tt: e.tensor_copy(
            out=gateT[:, tt * 128:(tt + 1) * 128],
            in_=psB_bf[0:64, 0:128]),
            reads=[c.psB_lo_b], writes=[gateT_b])

    for s2 in range(2):
        def aev(j, ps, ps_b, s2=s2):
            t0 = s2 * SLAB
            prog.op("vector", lambda e, j=j, t0=t0: e.tensor_tensor(
                out=zg[0:64, j, t0:t0 + SLAB], in0=ps[0:64, 0:SLAB],
                in1=gateT[:, t0:t0 + SLAB], op=ALU.mult),
                reads=list(ps_b) + [gateT_b], writes=[zg_bufs[j]])
            prog.op("vector", lambda e, j=j, t0=t0: e.tensor_tensor(
                out=zg[64:128, j, t0:t0 + SLAB], in0=ps[64:128, 0:SLAB],
                in1=gateT[:, t0:t0 + SLAB], op=ALU.mult),
                reads=list(ps_b) + [gateT_b], writes=[zg_bufs[j]])
        emit_wproj(c, io["A_wT"], io["A_wT__b"],
                   xnT[:, :, s2 * SLAB:(s2 + 1) * SLAB], xnT_bufs, SLAB,
                   list(range(32)), list(range(4)), aev)

    for oh in range(4):
        for ch in range(4):
            prog.op("sync", lambda e, ch=ch, oh=oh: e.dma_start(
                out=c.lnn[0][:].bitcast(F32)[:, 0:1024],
                in_=io["B_wT"][ch * 128:(ch + 1) * 128,
                               oh * 1024:(oh + 1) * 1024]),
                reads=[io["B_wT__b"]], writes=[c.lnn[1]], dma="bw")
            prog.op("vector", lambda e, ch=ch: e.tensor_copy(
                out=bwb[:, ch, :],
                in_=c.lnn[0][:].bitcast(F32)[:, 0:1024]),
                reads=[c.lnn[1]], writes=[bwb_bufs[ch]])
        for tt in range(16):
            xr, xr_b = c.lnx
            prog.op("gpsimd", lambda e, tt=tt, oh=oh, xr=xr: e.dma_start(
                out=xr[:, 0:1024],
                in_=x_d[tt * 128:(tt + 1) * 128, oh * 1024:(oh + 1) * 1024]),
                reads=[x_b], writes=[xr_b], dma="xr")
            for nch in range(4):
                stat = zg[:, nch, tt * 128:(tt + 1) * 128]
                for oc in range(2):
                    mov = bwb[:, nch, oc * 512:(oc + 1) * 512]
                    dstp = c.psA[:, oc * 512:(oc + 1) * 512]
                    prog.op("tensor",
                            lambda e, stat=stat, mov=mov, dstp=dstp, nch=nch:
                            e.matmul(dstp, stat, mov, start=(nch == 0),
                                     stop=(nch == 3)),
                            reads=[zg_bufs[nch], bwb_bufs[nch]],
                            writes=[c.psA_lo_b])
            prog.op("vector", lambda e, xr=xr: e.tensor_tensor(
                out=x2s[:, 0:1024], in0=c.psA[:, 0:1024], in1=xr[:, 0:1024],
                op=ALU.add),
                reads=[c.psA_lo_b, xr_b], writes=[x2s_b])
            prog.op("gpsimd", lambda e, tt=tt, oh=oh: e.dma_start(
                out=out_d[tt * 128:(tt + 1) * 128,
                          oh * 1024:(oh + 1) * 1024], in_=x2s[:, 0:1024]),
                reads=[x2s_b], writes=[out_b], dma="x1o")


def emit_l1_attn(c, R, io):
    prog = c.p
    prog.scope = "s3.pre"
    x_d, x_b = io["x2_sc"], io["x2_sc__b"]
    psB_bf = c.psB[:].bitcast(BF16)
    m_ = 0.1 * math.log(128.0) + 1.0
    scale = (128.0 ** -0.5) * m_ * m_
    R.reset()
    ckvT, ckvT_bufs = R.carve_chunks("ckvT", 2, T, BF16)
    kpeT, kpeT_b = R.carve("kpeT", [64, T], BF16)
    qanT, qanT_bufs = R.carve_chunks("qanT", 8, T, BF16)
    qh_l = [R.carve(f"l1qh{i}", [128, T], BF16) for i in range(2)]
    vh_l = [R.carve(f"l1vh{i}", [128, T], BF16) for i in range(2)]
    chkA = R.checkpoint()
    cosT_s, cosT_b = R.carve("cosTs", [64, T], BF16)
    sinT_s, sinT_b = R.carve("sinTs", [64, T], BF16)
    rp, rp_b = R.carve("rp", [64, T], BF16)
    t2, t2_b = R.carve("t2", [64, T], BF16)
    o1, o1_b = R.carve("o1", [64, T], BF16)
    qln_s, qln_b = R.carve("qlns", [128, 1024], BF16)
    kvln_s, kvln_b = R.carve("kvlns", [128, 256], BF16)
    chk1 = R.checkpoint()

    prog.op("sync", lambda e: e.dma_start(
        out=c.lnx[0][:, 0:1024], in_=io["q_ln"].broadcast_to([128, 1024])),
        reads=[io["q_ln__b"]], writes=[c.lnx[1]], dma="cst2")
    prog.op("vector", lambda e: e.tensor_copy(
        out=qln_s[:], in_=c.lnx[0][:, 0:1024]),
        reads=[c.lnx[1]], writes=[qln_b])
    prog.op("sync", lambda e: e.dma_start(
        out=c.lnx[0][:, 0:256], in_=io["kv_ln"].broadcast_to([128, 256])),
        reads=[io["kv_ln__b"]], writes=[c.lnx[1]], dma="cst3")
    prog.op("vector", lambda e: e.tensor_copy(
        out=kvln_s[:], in_=c.lnx[0][:, 0:256]),
        reads=[c.lnx[1]], writes=[kvln_b])
    prog.op("sync", lambda e: e.dma_start(out=cosT_s[:], in_=io["cosT"]),
            reads=[io["cosT__b"]], writes=[cosT_b], dma="cst2")
    prog.op("sync", lambda e: e.dma_start(out=sinT_s[:], in_=io["sinT"]),
            reads=[io["sinT__b"]], writes=[sinT_b], dma="cst2")
    tok = prog.dma_token("cst2")
    for b in (qln_b, kvln_b, cosT_b, sinT_b):
        b.w_tok = tok

    def rms_pass(srcv, src_bufs, nch, gain_s, gain_b, dstv, dst_bufs, t_off,
                 nat, nat_b):
        for tl in range(SLAB // 128):
            for j in range(nch):
                prog.op("tensor", lambda e, j=j, tl=tl, srcv=srcv: e.transpose(
                    psB_bf[:, j * 128:(j + 1) * 128],
                    srcv[:, j, tl * 128:(tl + 1) * 128], c.ident[:]),
                    reads=[src_bufs[j], c.ident_b], writes=[c.psB_lo_b])
            src = psB_bf[:, 0:nch * 128].rearrange("p (j t) -> p j t", j=nch)
            prog.op("vector", lambda e, src=src, nat=nat, nch=nch:
                    e.tensor_copy(
                        out=nat[:, 0:nch * 128].rearrange(
                            "p (j t) -> p j t", j=nch),
                        in_=src),
                    reads=[c.psB_lo_b], writes=[nat_b])
            nst = max(1, nch * 128 // 512)
            w = min(512, nch * 128)
            for ch in range(nst):
                prog.op("vector", lambda e, ch=ch, w=w, nat=nat: e.bn_stats(
                    out=c.lnstats[0][:, ch * 6:(ch + 1) * 6],
                    in_=nat[:, ch * w:(ch + 1) * w]),
                    reads=[nat_b], writes=[c.lnstats[1]])
            prog.op("vector", lambda e, nst=nst: e.bn_aggr(
                out=c.lnmv[0][:], in_=c.lnstats[0][:, 0:nst * 6]),
                reads=[c.lnstats[1]], writes=[c.lnmv[1]])
            prog.op("vector", lambda e: e.tensor_tensor(
                out=c.msq[0][:], in0=c.lnmv[0][:, 0:1],
                in1=c.lnmv[0][:, 0:1], op=ALU.mult),
                reads=[c.lnmv[1]], writes=[c.msq[1]])
            prog.op("vector", lambda e: e.tensor_tensor(
                out=c.msq[0][:], in0=c.msq[0][:], in1=c.lnmv[0][:, 1:2],
                op=ALU.add), reads=[c.msq[1], c.lnmv[1]], writes=[c.msq[1]])
            prog.op("scalar", lambda e: e.activation(
                out=c.rrms[0][:], in_=c.msq[0][:], func=AF.Ln,
                bias=c.eps6[0][:]),
                reads=[c.msq[1], c.eps6[1]], writes=[c.rrms[1]])
            prog.op("scalar", lambda e: e.activation(
                out=c.rrms[0][:], in_=c.rrms[0][:], func=AF.Exp, scale=-0.5),
                reads=[c.rrms[1]], writes=[c.rrms[1]])
            prog.op("vector", lambda e, nat=nat, nch=nch: e.tensor_scalar_mul(
                out=nat[:, 0:nch * 128], in0=nat[:, 0:nch * 128],
                scalar1=c.rrms[0][:]),
                reads=[nat_b, c.rrms[1]], writes=[nat_b])
            prog.op("vector", lambda e, nat=nat, nch=nch, gain_s=gain_s:
                    e.tensor_mul(
                        out=nat[:, 0:nch * 128], in0=nat[:, 0:nch * 128],
                        in1=gain_s[:, 0:nch * 128]),
                    reads=[nat_b, gain_b], writes=[nat_b])
            for j in range(nch):
                prog.op("tensor", lambda e, j=j, nat=nat: e.transpose(
                    psB_bf[:, j * 128:(j + 1) * 128],
                    nat[:, j * 128:(j + 1) * 128], c.ident[:]),
                    reads=[nat_b, c.ident_b], writes=[c.psB_lo_b])
            src2 = psB_bf[:, 0:nch * 128].rearrange("p (j t) -> p j t", j=nch)
            dst = dstv[:, :, t_off + tl * 128:t_off + (tl + 1) * 128]
            prog.op("vector", lambda e, src2=src2, dst=dst: e.tensor_copy(
                out=dst, in_=src2), reads=[c.psB_lo_b], writes=dst_bufs)

    for s2 in range(2):
        R.reset_to(chk1)
        xnT, xnT_bufs = R.carve_chunks(f"s3xnT{s2}", 32, SLAB, BF16)
        qaT, qaT_bufs = R.carve_chunks(f"qaT{s2}", 8, SLAB, BF16)
        ckaT, ckaT_bufs = R.carve_chunks(f"ckaT{s2}", 2, SLAB, BF16)
        qa_nat = R.carve(f"qanat{s2}", [128, 1024], BF16)
        ck_nat = R.carve(f"cknat{s2}", [128, 256], BF16)
        emit_ln_slab(c, x_d, x_b, s2, xnT, xnT_bufs)
        prog.scope = "s3.aproj"

        def qaev(j, ps, ps_b, qaT=qaT, qaT_bufs=qaT_bufs):
            dst = qaT[:, j, :]
            prog.op("vector", lambda e, dst=dst, j=j: e.tensor_copy(
                out=dst, in_=ps[:, 0:SLAB]),
                reads=list(ps_b), writes=[qaT_bufs[j]])
        emit_wproj(c, io["wq_aT"], io["wq_aT__b"], xnT, xnT_bufs, SLAB,
                   list(range(32)), list(range(8)), qaev)

        def ckev(j, ps, ps_b, ckaT=ckaT, ckaT_bufs=ckaT_bufs):
            dst = ckaT[:, j, :]
            prog.op("vector", lambda e, dst=dst, j=j: e.tensor_copy(
                out=dst, in_=ps[:, 0:SLAB]),
                reads=list(ps_b), writes=[ckaT_bufs[j]])
        emit_wproj(c, io["wkv_aT"], io["wkv_aT__b"], xnT, xnT_bufs, SLAB,
                   list(range(32)), list(range(2)), ckev)

        wf, wf_b = c.wst_wf
        w0, w0_b = c.wst_w0
        for ki in range(32):
            prog.op("sync", lambda e, ki=ki: e.dma_start(
                out=wf[:, 0:64],
                in_=io["wkv_aT"][ki * 128:(ki + 1) * 128, 256:320]),
                reads=[io["wkv_aT__b"]], writes=[wf_b], dma="wst0")
            prog.op("vector", lambda e: e.tensor_copy(
                out=w0[:, 0:64], in_=wf[:, 0:64]),
                reads=[wf_b], writes=[w0_b])
            for tc2 in range(2):
                prog.op("tensor", lambda e, ki=ki, tc2=tc2, xnT=xnT: e.matmul(
                    c.psC[0:64, tc2 * 512:(tc2 + 1) * 512], w0[:, 0:64],
                    xnT[:, ki, tc2 * 512:(tc2 + 1) * 512],
                    start=(ki == 0), stop=(ki == 31)),
                    reads=[w0_b, xnT_bufs[ki]],
                    writes=[c.psC_lo_b if tc2 == 0 else c.psC_hi_b])
        s_off = s2 * SLAB
        prog.op("vector", lambda e: e.tensor_copy(
            out=rp[:, 0:SLAB], in_=c.psC[0:64, 0:SLAB]),
            reads=[c.psC_lo_b, c.psC_hi_b], writes=[rp_b])
        prog.op("vector", lambda e: e.tensor_copy(
            out=t2[0:32, 0:SLAB], in_=rp[32:64, 0:SLAB]),
            reads=[rp_b], writes=[t2_b])
        prog.op("vector", lambda e: e.tensor_copy(
            out=t2[32:64, 0:SLAB], in_=rp[0:32, 0:SLAB]),
            reads=[rp_b], writes=[t2_b])
        prog.op("vector", lambda e, s_off=s_off: e.tensor_mul(
            out=o1[:, 0:SLAB], in0=rp[:, 0:SLAB],
            in1=cosT_s[:, s_off:s_off + SLAB]),
            reads=[rp_b, cosT_b], writes=[o1_b])
        prog.op("vector", lambda e, s_off=s_off: e.tensor_mul(
            out=t2[:, 0:SLAB], in0=t2[:, 0:SLAB],
            in1=sinT_s[:, s_off:s_off + SLAB]),
            reads=[t2_b, sinT_b], writes=[t2_b])
        prog.op("vector", lambda e, s_off=s_off: e.tensor_tensor(
            out=kpeT[:, s_off:s_off + SLAB], in0=o1[:, 0:SLAB],
            in1=t2[:, 0:SLAB], op=ALU.add),
            reads=[o1_b, t2_b], writes=[kpeT_b])

        rms_pass(qaT, qaT_bufs, 8, qln_s, qln_b, qanT, qanT_bufs, s_off,
                 *qa_nat)
        rms_pass(ckaT, ckaT_bufs, 2, kvln_s, kvln_b, ckvT, ckvT_bufs, s_off,
                 *ck_nat)

    prog.scope = "s3.wqb"
    R.reset_to(chk1)
    for s2 in range(2):
        t0q = s2 * SLAB

        def qev(j, ps, ps_b, t0q=t0q):
            qs, qs_b = qh_l[j % 2]
            prog.op("vector", lambda e, qs=qs: e.tensor_copy(
                out=qs[:, 0:SLAB], in_=ps[:, 0:SLAB]),
                reads=list(ps_b), writes=[qs_b])
            prog.op("vector", lambda e, qs=qs: e.tensor_copy(
                out=rp[:, 0:SLAB], in_=qs[64:128, 0:SLAB]),
                reads=[qs_b], writes=[rp_b])
            prog.op("vector", lambda e: e.tensor_copy(
                out=t2[0:32, 0:SLAB], in_=rp[32:64, 0:SLAB]),
                reads=[rp_b], writes=[t2_b])
            prog.op("vector", lambda e: e.tensor_copy(
                out=t2[32:64, 0:SLAB], in_=rp[0:32, 0:SLAB]),
                reads=[rp_b], writes=[t2_b])
            prog.op("vector", lambda e, t0q=t0q: e.tensor_mul(
                out=o1[:, 0:SLAB], in0=rp[:, 0:SLAB],
                in1=cosT_s[:, t0q:t0q + SLAB]),
                reads=[rp_b, cosT_b], writes=[o1_b])
            prog.op("vector", lambda e, t0q=t0q: e.tensor_mul(
                out=t2[:, 0:SLAB], in0=t2[:, 0:SLAB],
                in1=sinT_s[:, t0q:t0q + SLAB]),
                reads=[t2_b, sinT_b], writes=[t2_b])
            prog.op("vector", lambda e, qs=qs: e.tensor_tensor(
                out=qs[64:128, 0:SLAB], in0=o1[:, 0:SLAB],
                in1=t2[:, 0:SLAB], op=ALU.add),
                reads=[o1_b, t2_b], writes=[qs_b])
            prog.op("gpsimd", lambda e, j=j, qs=qs, t0q=t0q: e.dma_start(
                out=io["q_sc"][j][:, t0q:t0q + SLAB], in_=qs[:, 0:SLAB]),
                reads=[qs_b], writes=[io["q_sc__b"]], dma="qs%d" % (j % 2))
        emit_wproj(c, io["wq_bT"], io["wq_bT__b"],
                   qanT[:, :, t0q:t0q + SLAB], qanT_bufs, SLAB,
                   list(range(8)), list(range(32)), qev)

    prog.scope = "s3.kv"
    R.reset_to(chkA)
    knT, knT_bufs = R.carve_chunks("knT", 16, T, BF16)
    kf, kf_b = R.carve("kf", [128, T], BF16)
    vhn, vhn_b = R.carve("vhn", [128, 16 * 128], BF16)
    vhnv = vhn.rearrange("p (d t) -> p d t", d=16)
    pbv, pb_bl = R.carve_chunks("l1pbuf", 2, 4 * T, BF16)
    pbufs = [(pbv[:, 0, :], pb_bl[0]), (pbv[:, 1, :], pb_bl[1])]
    ptb, ptb_bufs = R.carve_chunks("l1ptbuf", 3, 512, BF16)

    for s2 in range(2):
        t0q = s2 * SLAB

        def kev(j, ps, ps_b, t0q=t0q):
            dst = knT[:, j, t0q:t0q + SLAB]
            prog.op("vector", lambda e, dst=dst, j=j: e.tensor_copy(
                out=dst, in_=ps[:, 0:SLAB]), reads=list(ps_b),
                writes=[knT_bufs[j]])
        emit_wproj(c, io["wkv_b_kT"], io["wkv_b_kT__b"],
                   ckvT[:, :, t0q:t0q + SLAB], ckvT_bufs, SLAB,
                   list(range(2)), list(range(16)), kev)

        def vev(j, ps, ps_b, t0q=t0q):
            vs, vs_b = vh_l[j % 2]
            prog.op("vector", lambda e, vs=vs: e.tensor_copy(
                out=vs[:, 0:SLAB], in_=ps[:, 0:SLAB]),
                reads=list(ps_b), writes=[vs_b])
            prog.op("gpsimd", lambda e, j=j, vs=vs, t0q=t0q: e.dma_start(
                out=io["v_sc"][j][:, t0q:t0q + SLAB], in_=vs[:, 0:SLAB]),
                reads=[vs_b], writes=[io["v_sc__b"]], dma="vs%d" % (j % 2))
        emit_wproj(c, io["wkv_b_vT"], io["wkv_b_vT__b"],
                   ckvT[:, :, t0q:t0q + SLAB], ckvT_bufs, SLAB,
                   list(range(2)), list(range(32)), vev)

    def q_get(qh):
        qt_s, qt_sb = qh_l[qh % 2]
        prog.op("gpsimd", lambda e, qh=qh, qt_s=qt_s: e.dma_start(
            out=qt_s[:], in_=io["q_sc"][qh]),
            reads=[io["q_sc__b"]], writes=[qt_sb], dma="ql%d" % (qh % 2))
        return qt_s, qt_sb

    def kT_get(qh):
        kno = knT[:, qh // 2, :]
        prog.op("vector", lambda e, qh=qh, kno=kno: e.tensor_copy(
            out=kf[0:64, :], in_=kno[(qh % 2) * 64:(qh % 2) * 64 + 64, :]),
            reads=[knT_bufs[qh // 2]], writes=[kf_b])
        prog.op("vector", lambda e: e.tensor_copy(
            out=kf[64:128, :], in_=kpeT[:]),
            reads=[kpeT_b], writes=[kf_b])
        return kf, kf_b

    def v_get(qh):
        vt_s, vt_sb = vh_l[qh % 2]
        prog.op("gpsimd", lambda e, qh=qh, vt_s=vt_s: e.dma_start(
            out=vt_s[:], in_=io["v_sc"][qh]),
            reads=[io["v_sc__b"]], writes=[vt_sb], dma="vl%d" % (qh % 2))
        for g8 in range(2):
            for tl2 in range(8):
                prog.op("tensor", lambda e, g8=g8, tl2=tl2, vt_s=vt_s:
                        e.transpose(
                            psB_bf[:, tl2 * 128:(tl2 + 1) * 128],
                            vt_s[:, (g8 * 8 + tl2) * 128:
                                 (g8 * 8 + tl2 + 1) * 128], c.ident[:]),
                        reads=[vt_sb, c.ident_b], writes=[c.psB_lo_b])
            src = psB_bf[:, 0:1024].rearrange("p (j t) -> p j t", j=8)
            dst = vhnv[:, g8 * 8:(g8 + 1) * 8, :]
            prog.op("vector", lambda e, src=src, dst=dst: e.tensor_copy(
                out=dst, in_=src), reads=[c.psB_lo_b], writes=[vhn_b])
        return (lambda kt: vhnv[:, kt, :]), [vhn_b]

    prog.scope = "s3.attn"
    emit_attn_heads(c, R, kT_get, v_get, q_get, scale,
                    io["hv_sc"], io["hv_sc__b"],
                    pbufs, ptb, ptb_bufs, [0, 1, 2, 3])

    prog.scope = "s3.wo"
    R.reset()
    emit_wo_residual(c, R, io, "woT", x_d, x_b,
                     io["x3_sc"], io["x3_sc__b"])


def emit_l1_ffn(c, R, io):
    prog = c.p
    prog.scope = "s4.zs"
    x_d, x_b = io["x3_sc"], io["x3_sc__b"]
    zsg_d, zsg_b = io["zs_gu"], io["zs_gu__b"]
    zsd_d, zsd_b = io["zs_dn"], io["zs_dn__b"]
    wgu_d, wgu_b = io["wgu_deq"], io["wgu_deq__b"]
    wdn_d, wdn_b = io["wdn_deq"], io["wdn_deq__b"]
    xns_d, xns_b = io["xns"], io["xns__b"]
    hs_d, hs_b = io["hs"], io["hs__b"]
    hsc_d, hsc_b = io["h_sc"], io["h_sc__b"]
    R.reset()
    xnsT, xnsT_b = R.carve("xnsT", [32, T], BF16)
    hsT, hsT_b = R.carve("hsT", [86, 512], BF16)
    zspan, zspan_b = R.carve("zspan", [32, 1024], BF16)
    zdn, zdn_b = R.carve("zdn", [86, 1024], BF16)
    sg_stage = [R.carve(f"sgst{i}", [128, 1024], BF16) for i in range(2)]
    hstage = [R.carve(f"hst{i}", [128, 1024], BF16) for i in range(2)]
    chk0 = R.checkpoint()

    zw, zw_b = R.carve("zw", [86, 2752], I32)
    zn, zn_b = R.carve("zn", [86, 5504], I32)
    zs_s, zs_s_b = R.carve("zs_s", [86, 5504], F32)
    zs_o, zs_o_b = R.carve("zs_o", [86, 5504], BF16)
    prog.op("sync", lambda e: e.dma_start(out=zw[0:32, :], in_=io["qz_gu"]),
            reads=[io["qz_gu__b"]], writes=[zw_b], dma="zw")
    for cc in range(4):
        for k in range(8):
            prog.op("vector", lambda e, cc=cc, k=k: e.tensor_scalar(
                out=zn[0:32, k::8], in0=zw[0:32, cc * 688:(cc + 1) * 688],
                scalar1=4 * k, scalar2=15,
                op0=ALU.logical_shift_right, op1=ALU.bitwise_and),
                reads=[zw_b], writes=[zn_b])
        prog.op("sync", lambda e, cc=cc: e.dma_start(
            out=zs_s[0:32, :], in_=io["s_gu"][:, cc * 5504:(cc + 1) * 5504]),
            reads=[io["s_gu__b"]], writes=[zs_s_b], dma="zss")
        prog.op("vector", lambda e: e.tensor_tensor(
            out=zs_o[0:32, :], in0=zn[0:32, :], in1=zs_s[0:32, :],
            op=ALU.mult), reads=[zn_b, zs_s_b], writes=[zs_o_b])
        prog.op("gpsimd", lambda e, cc=cc: e.dma_start(
            out=zsg_d[:, cc * 5504:(cc + 1) * 5504], in_=zs_o[0:32, :]),
            reads=[zs_o_b], writes=[zsg_b], dma="zso")
    prog.op("sync", lambda e: e.dma_start(out=zw[0:86, 0:512], in_=io["qz_dn"]),
            reads=[io["qz_dn__b"]], writes=[zw_b], dma="zw")
    for k in range(8):
        prog.op("vector", lambda e, k=k: e.tensor_scalar(
            out=zn[0:86, k:4096:8], in0=zw[0:86, 0:512],
            scalar1=4 * k, scalar2=15,
            op0=ALU.logical_shift_right, op1=ALU.bitwise_and),
            reads=[zw_b], writes=[zn_b])
    prog.op("sync", lambda e: e.dma_start(
        out=zs_s[0:86, 0:4096], in_=io["s_dn"]),
        reads=[io["s_dn__b"]], writes=[zs_s_b], dma="zss")
    prog.op("vector", lambda e: e.tensor_tensor(
        out=zs_o[0:86, 0:4096], in0=zn[0:86, 0:4096],
        in1=zs_s[0:86, 0:4096], op=ALU.mult),
        reads=[zn_b, zs_s_b], writes=[zs_o_b])
    prog.op("gpsimd", lambda e: e.dma_start(
        out=zsd_d, in_=zs_o[0:86, 0:4096]),
        reads=[zs_o_b], writes=[zsd_b], dma="zso")

    prog.scope = "s4.deq"
    shl_g, shl_g_b = io["shl_gu"], io["shl_gu__b"]
    shl_d, shl_d_b = io["shl_dn"], io["shl_dn__b"]
    R.reset_to(chk0)
    sf, sf_b = R.carve("sf", [86, 5504], F32)
    shi, shi_b = R.carve("shi", [86, 5504], BF16)
    shf, shf_b = R.carve("shf", [86, 5504], F32)
    slo, slo_b = R.carve("slo", [86, 5504], BF16)

    def emit_split(nr, w, src_ap, src_b, dst_hi, dst_lo):
        prog.op("sync", lambda e: e.dma_start(out=sf[0:nr, 0:w], in_=src_ap),
                reads=[src_b], writes=[sf_b], dma="shl")
        prog.op("vector", lambda e: e.tensor_copy(
            out=shi[0:nr, 0:w], in_=sf[0:nr, 0:w]),
            reads=[sf_b], writes=[shi_b])
        prog.op("vector", lambda e: e.tensor_copy(
            out=shf[0:nr, 0:w], in_=shi[0:nr, 0:w]),
            reads=[shi_b], writes=[shf_b])
        prog.op("vector", lambda e: e.tensor_tensor(
            out=shf[0:nr, 0:w], in0=sf[0:nr, 0:w], in1=shf[0:nr, 0:w],
            op=ALU.subtract), reads=[sf_b, shf_b], writes=[shf_b])
        prog.op("vector", lambda e: e.tensor_copy(
            out=slo[0:nr, 0:w], in_=shf[0:nr, 0:w]),
            reads=[shf_b], writes=[slo_b])
        prog.op("gpsimd", lambda e, dst_hi=dst_hi: e.dma_start(
            out=dst_hi, in_=shi[0:nr, 0:w]),
            reads=[shi_b], writes=[shl_g_b], dma="shlo")
        prog.op("gpsimd", lambda e, dst_lo=dst_lo: e.dma_start(
            out=dst_lo, in_=slo[0:nr, 0:w]),
            reads=[slo_b], writes=[shl_g_b], dma="shlo")
    for cc in range(4):
        emit_split(32, 5504, io["s_gu"][:, cc * 5504:(cc + 1) * 5504],
                   io["s_gu__b"],
                   shl_g[0][:, cc * 5504:(cc + 1) * 5504],
                   shl_g[1][:, cc * 5504:(cc + 1) * 5504])
    emit_split(86, 4096, io["s_dn"], io["s_dn__b"], shl_d[0], shl_d[1])
    shl_d_b.w_tok = shl_g_b.w_tok = prog.dma_token("shlo")

    R.reset_to(chk0)
    qwl = [R.carve(f"qwrow{i}", [128, 2752], I32) for i in range(2)]
    wil = [R.carve(f"wint{i}", [128, 5504], I32) for i in range(2)]
    srl = [R.carve(f"srl{i}", [1, 11008], BF16) for i in range(2)]
    wdq, wdq_bufs = R.carve_chunks("wdq", 2, 11008, BF16)
    onesr, onesr_b = R.carve("onesr", [1, 128], BF16)
    prog.op("vector", lambda e: e.memset(onesr, 1.0), writes=[onesr_b])
    bslots = [(c.psA, 0, [c.psA_lo_b]), (c.psA, 1024, [c.psA_hi_b]),
              (c.psB, 0, [c.psB_lo_b, c.psB_hi_b]),
              (c.psC, 0, [c.psC_lo_b, c.psC_hi_b])]
    bs_ctr = [0]

    def emit_scaled_mult(wint, wint_b, srow, srow_b, o0, w, dst, dst_b,
                         lo_off):
        ps, off, pbl = bslots[bs_ctr[0] % 4]
        bs_ctr[0] += 1
        for half in range((w + 511) // 512):
            h0, h1 = half * 512, min(half * 512 + 512, w)
            hb = pbl[min(half, len(pbl) - 1)]
            for hl in range(2):
                prog.op("tensor",
                        lambda e, ps=ps, off=off, h0=h0, h1=h1, hl=hl,
                        srow=srow, o0=o0, lo_off=lo_off: e.matmul(
                            ps[:, off + h0:off + h1], onesr,
                            srow[0:1, hl * lo_off + o0 + h0:
                                 hl * lo_off + o0 + h1],
                            start=(hl == 0), stop=(hl == 1)),
                        reads=[onesr_b, srow_b], writes=[hb])
        prog.op("vector", lambda e, ps=ps, off=off, o0=o0, w=w, wint=wint,
                dst=dst: e.tensor_tensor(
            out=dst, in0=wint[:, o0:o0 + w], in1=ps[:, off:off + w],
            op=ALU.mult),
            reads=[wint_b] + pbl, writes=[dst_b])
    DENG = ("sync", "scalar", "gpsimd")
    for g in range(32):
        qwrow, qwrow_b = qwl[g % 2]
        prog.op(DENG[g % 2], lambda e, g=g, qwrow=qwrow: e.dma_start(
            out=qwrow[:], in_=io["qw_gu"][g * 128:(g + 1) * 128, :]),
            reads=[io["qw_gu__b"]], writes=[qwrow_b], dma=f"qw{g % 2}")
        for cc in range(4):
            wint, wint_b = wil[cc % 2]
            srow, srow_b = srl[cc % 2]
            hf = cc // 2
            for k in range(8):
                prog.op("vector", lambda e, cc=cc, k=k, wint=wint,
                        qwrow=qwrow: e.tensor_scalar(
                    out=wint[:, k::8], in0=qwrow[:, cc * 688:(cc + 1) * 688],
                    scalar1=4 * k, scalar2=15,
                    op0=ALU.logical_shift_right, op1=ALU.bitwise_and),
                    reads=[qwrow_b], writes=[wint_b])
            prog.op(DENG[cc % 3], lambda e, g=g, cc=cc, srow=srow:
                    e.dma_start(
                out=srow[0:1, :], in_=shl_g[:, g, cc * 5504:(cc + 1) * 5504]),
                reads=[shl_g_b], writes=[srow_b], dma=f"sbc{cc}")
            for (o0, w) in ((0, 1024), (1024, 1024), (2048, 1024),
                            (3072, 1024), (4096, 1024), (5120, 384)):
                dst = wdq[:, hf, (cc % 2) * 5504 + o0:
                          (cc % 2) * 5504 + o0 + w]
                emit_scaled_mult(wint, wint_b, srow, srow_b, o0, w,
                                 dst, wdq_bufs[hf], 5504)
        for hf in range(2):
            prog.op(DENG[(g + hf) % 3], lambda e, g=g, hf=hf: e.dma_start(
                out=wgu_d[g][:, hf * 11008:(hf + 1) * 11008],
                in_=wdq[:, hf, :]),
                reads=[wdq_bufs[hf]], writes=[wgu_b], dma=f"wdq{(g + hf) % 3}")
    for g in range(86):
        qwrow, qwrow_b = qwl[g % 2]
        wint, wint_b = wil[g % 2]
        srow, srow_b = srl[g % 2]
        prog.op(DENG[g % 2], lambda e, g=g, qwrow=qwrow: e.dma_start(
            out=qwrow[:, 0:512], in_=io["qw_dn"][g * 128:(g + 1) * 128, :]),
            reads=[io["qw_dn__b"]], writes=[qwrow_b], dma=f"qw{g % 2}")
        for k in range(8):
            prog.op("vector", lambda e, k=k, wint=wint, qwrow=qwrow:
                    e.tensor_scalar(
                out=wint[:, k:4096:8], in0=qwrow[:, 0:512],
                scalar1=4 * k, scalar2=15,
                op0=ALU.logical_shift_right, op1=ALU.bitwise_and),
                reads=[qwrow_b], writes=[wint_b])
        prog.op(DENG[g % 3], lambda e, g=g, srow=srow: e.dma_start(
            out=srow[0:1, 0:8192], in_=shl_d[:, g, :]),
            reads=[shl_d_b], writes=[srow_b], dma=f"sbc{g % 3}")
        for (o0, w) in ((0, 1024), (1024, 1024), (2048, 1024), (3072, 1024)):
            dst = wdq[:, g % 2, o0:o0 + w]
            emit_scaled_mult(wint, wint_b, srow, srow_b, o0, w,
                             dst, wdq_bufs[g % 2], 4096)
        prog.op(DENG[g % 3], lambda e, g=g: e.dma_start(
            out=wdn_d[g], in_=wdq[:, g % 2, 0:4096]),
            reads=[wdq_bufs[g % 2]], writes=[wdn_b], dma=f"wdq{g % 3}")

    prog.scope = "s4.gateup"
    ip_groups = [list(range(a, min(a + 2, 86))) for a in range(0, 86, 2)]
    for ts in range(2):
        R.reset_to(chk0)
        xnT, xnT_bufs = R.carve_chunks(f"s4xnT{ts}", 32, SLAB, BF16)
        spr = []
        for i in range(2):
            v, bufs = R.carve_chunks(f"spr{i}", 32, 512, BF16)
            spr.append((v, bufs))
        zspr = [R.carve(f"zspr{i}", [32, 512], BF16) for i in range(2)]
        emit_ln_slab(c, x_d, x_b, ts, xnT, xnT_bufs)
        for g in range(32):
            for tc in range(2):
                prog.op("tensor", lambda e, g=g, tc=tc, xnT=xnT: e.matmul(
                    c.psA[0:1, tc * 512:(tc + 1) * 512], c.onesn[0][:],
                    xnT[:, g, tc * 512:(tc + 1) * 512],
                    start=True, stop=True),
                    reads=[c.onesn[1], xnT_bufs[g]], writes=[c.psA_lo_b])
            prog.op("vector", lambda e: e.tensor_copy(
                out=c.xrow[0][:], in_=c.psA[0:1, 0:1024]),
                reads=[c.psA_lo_b], writes=[c.xrow[1]])
            prog.op("gpsimd", lambda e, g=g, ts=ts: e.dma_start(
                out=xns_d[g:g + 1, ts * SLAB:(ts + 1) * SLAB],
                in_=c.xrow[0][:]),
                reads=[c.xrow[1]], writes=[xns_b], dma="xns")
        prog.op("gpsimd", lambda e, ts=ts: e.dma_start(
            out=xnsT[:, 0:SLAB], in_=xns_d[:, ts * SLAB:(ts + 1) * SLAB]),
            reads=[xns_b], writes=[xnsT_b], dma="xnsl")
        zsg_v = zsg_d.rearrange("g (u n) -> g u n", u=2)
        for gi, ipg in enumerate(ip_groups):
            n_ip = len(ipg)
            ip0 = ipg[0]
            rb = gi % 2
            spw, sp_bufs = spr[rb]
            zs2, zs2_b = zspr[rb]
            prog.op("sync", lambda e, ip0=ip0, n_ip=n_ip, zs2=zs2:
                    e.dma_start(
                        out=zs2[:, 0:2 * n_ip * 128].rearrange(
                            "g (u a) -> g u a", u=2),
                        in_=zsg_v[:, :, ip0 * 128:(ip0 + n_ip) * 128]),
                    reads=[zsg_b], writes=[zs2_b], dma="zsl%d" % rb)
            for hg in range(32):
                eng = DENG[hg % 3]
                q = "spg%d" % (hg % 9)
                src = wgu_d[hg].rearrange(
                    "p (u n) -> p u n", u=2)[:, :, ip0 * 128:(ip0 + n_ip) * 128]
                dst = spw[:, hg, 0:2 * n_ip * 128].rearrange(
                    "p (u a) -> p u a", u=2)
                prog.op(eng, lambda e, src=src, dst=dst: e.dma_start(
                    out=dst, in_=src),
                    reads=[wgu_b], writes=[sp_bufs[hg]], dma=q)
            for i2, ip in enumerate(ipg):
                for gu in range(2):
                    pst, o0 = ((c.psA, i2 * 1024) if gu == 0 else
                               ((c.psB, 0) if i2 == 0 else (c.psC, 0)))
                    zoff = gu * 256 + i2 * 128
                    for tc in range(2):
                        pb = ((c.psA_lo_b if i2 == 0 else c.psA_hi_b)
                              if gu == 0 else
                              ((c.psB_lo_b, c.psB_hi_b)[tc] if i2 == 0
                               else (c.psC_lo_b, c.psC_hi_b)[tc]))
                        prog.op("tensor",
                                lambda e, pst=pst, o0=o0, tc=tc, zoff=zoff,
                                zs2=zs2: e.matmul(
                                    pst[:, o0 + tc * 512:o0 + (tc + 1) * 512],
                                    zs2[:, zoff:zoff + 128],
                                    xnsT[:, tc * 512:(tc + 1) * 512],
                                    start=True, stop=False),
                                reads=[zs2_b, xnsT_b], writes=[pb])
            for hg in range(32):
                for i2, ip in enumerate(ipg):
                    for gu in range(2):
                        pst, o0 = ((c.psA, i2 * 1024) if gu == 0 else
                                   ((c.psB, 0) if i2 == 0 else (c.psC, 0)))
                        soff = gu * 256 + i2 * 128
                        last = (hg == 31)
                        for tc in range(2):
                            pb = ((c.psA_lo_b if i2 == 0 else c.psA_hi_b)
                                  if gu == 0 else
                                  ((c.psB_lo_b, c.psB_hi_b)[tc] if i2 == 0
                                   else (c.psC_lo_b, c.psC_hi_b)[tc]))
                            prog.op(
                                "tensor",
                                lambda e, pst=pst, o0=o0, tc=tc, soff=soff,
                                hg=hg, last=last, xnT=xnT, spw=spw: e.matmul(
                                    pst[:, o0 + tc * 512:o0 + (tc + 1) * 512],
                                    spw[:, hg, soff:soff + 128],
                                    xnT[:, hg, tc * 512:(tc + 1) * 512],
                                    start=False, stop=last),
                                reads=[sp_bufs[hg], xnT_bufs[hg]],
                                writes=[pb])
            for i2, ip in enumerate(ipg):
                hs_s, hs_sb = hstage[i2]
                upst = c.psB if i2 == 0 else c.psC
                upb = ([c.psB_lo_b, c.psB_hi_b] if i2 == 0
                       else [c.psC_lo_b, c.psC_hi_b])
                sg_t, sg_tb = sg_stage[i2]
                prog.op("scalar", lambda e, i2=i2, sg_t=sg_t: e.activation(
                    out=sg_t[:], in_=c.psA[:, i2 * 1024:(i2 + 1) * 1024],
                    func=AF.Silu),
                    reads=[c.psA_lo_b if i2 == 0 else c.psA_hi_b],
                    writes=[sg_tb])
                prog.op("vector", lambda e, sg_t=sg_t, upst=upst, hs_s=hs_s:
                        e.tensor_tensor(
                            out=hs_s[:], in0=sg_t[:], in1=upst[:, 0:1024],
                            op=ALU.mult),
                        reads=[sg_tb] + upb, writes=[hs_sb])
                prog.op("gpsimd", lambda e, ip=ip, ts=ts, hs_s=hs_s:
                        e.dma_start(out=hsc_d[ip, ts], in_=hs_s[:]),
                        reads=[hs_sb], writes=[hsc_b], dma="hso%d" % i2)

    R.reset_to(chk0)
    hslab, hslab_bufs = R.carve_chunks("hslab", 86, 512, BF16)
    wmov = [R.carve(f"wmov{i}", [128, 1024], BF16) for i in range(4)]
    xr4l = [R.carve(f"xr4{i}", [128, 1024], F32) for i in range(2)]
    x4sl = [R.carve(f"x4s{i}", [128, 1024], F32) for i in range(2)]
    pslots = [(c.psA, [c.psA_lo_b, c.psA_lo_b], 0),
              (c.psA, [c.psA_hi_b, c.psA_hi_b], 1024),
              (c.psB, [c.psB_lo_b, c.psB_hi_b], 0),
              (c.psC, [c.psC_lo_b, c.psC_hi_b], 0)]
    for tsl in range(4):
        ts2, half = tsl // 2, tsl % 2
        for qd in range(4):
            i0 = qd * 22
            i1 = min(i0 + 22, 86)
            src = hsc_d[i0:i1, ts2, :, half * 512:(half + 1) * 512].rearrange(
                "i p a -> p i a")
            prog.op(("gpsimd", "sync")[qd % 2],
                    lambda e, src=src, i0=i0, i1=i1: e.dma_start(
                        out=hslab[:, i0:i1, :], in_=src),
                    reads=[hsc_b],
                    writes=[hslab_bufs[i] for i in range(i0, i1)],
                    dma="hsl%d" % (qd % 2))
        for i in range(86):
            prog.op("tensor", lambda e, i=i: e.matmul(
                c.psA[0:1, 0:512], c.onesn[0][:], hslab[:, i, :],
                start=True, stop=True),
                reads=[c.onesn[1], hslab_bufs[i]], writes=[c.psA_lo_b])
            prog.op("vector", lambda e: e.tensor_copy(
                out=c.hrow[0][:], in_=c.psA[0:1, 0:512]),
                reads=[c.psA_lo_b], writes=[c.hrow[1]])
            prog.op("gpsimd", lambda e, i=i, tsl=tsl: e.dma_start(
                out=hs_d[i:i + 1, tsl * 512:(tsl + 1) * 512],
                in_=c.hrow[0][:]),
                reads=[c.hrow[1]], writes=[hs_b], dma="hss")
        prog.op("gpsimd", lambda e, tsl=tsl: e.dma_start(
            out=hsT[:], in_=hs_d[:, tsl * 512:(tsl + 1) * 512]),
            reads=[hs_b], writes=[hsT_b], dma="hstl")
        for oq in range(4):
            prog.op("gpsimd", lambda e, oq=oq: e.dma_start(
                out=zdn[:], in_=zsd_d[:, oq * 1024:(oq + 1) * 1024]),
                reads=[zsd_b], writes=[zdn_b], dma="zdnl")
            for tl in range(4):
                pst, pbl, o0 = pslots[tl]
                for oc in range(2):
                    prog.op("tensor",
                            lambda e, pst=pst, o0=o0, oc=oc, tl=tl: e.matmul(
                                pst[:, o0 + oc * 512:o0 + (oc + 1) * 512],
                                hsT[:, tl * 128:(tl + 1) * 128],
                                zdn[:, oc * 512:(oc + 1) * 512],
                                start=True, stop=False),
                            reads=[hsT_b, zdn_b], writes=[pbl[oc]])
            for i in range(86):
                wm, wm_b = wmov[i % 4]
                prog.op(("sync", "scalar")[i % 2],
                        lambda e, i=i, oq=oq, wm=wm: e.dma_start(
                    out=wm[:], in_=wdn_d[i, :, oq * 1024:(oq + 1) * 1024]),
                    reads=[wdn_b], writes=[wm_b], dma=f"wdl{i % 6}")
                for tl in range(4):
                    pst, pbl, o0 = pslots[tl]
                    last = (i == 85)
                    for oc in range(2):
                        prog.op("tensor",
                                lambda e, pst=pst, o0=o0, oc=oc, i=i, tl=tl,
                                wm=wm, last=last: e.matmul(
                                    pst[:, o0 + oc * 512:o0 + (oc + 1) * 512],
                                    hslab[:, i, tl * 128:(tl + 1) * 128],
                                    wm[:, oc * 512:(oc + 1) * 512],
                                    start=False, stop=last),
                                reads=[hslab_bufs[i], wm_b],
                                writes=[pbl[oc]])
            for tl in range(4):
                pst, pbl, o0 = pslots[tl]
                tt = tsl * 4 + tl
                xr4, xr4_b = xr4l[tl % 2]
                x4s, x4s_b = x4sl[tl % 2]
                prog.op("gpsimd", lambda e, tt=tt, oq=oq, xr4=xr4:
                        e.dma_start(
                            out=xr4[:], in_=x_d[tt * 128:(tt + 1) * 128,
                                                oq * 1024:(oq + 1) * 1024]),
                        reads=[x_b], writes=[xr4_b], dma="xr%d" % (tl % 2))
                prog.op("vector", lambda e, pst=pst, o0=o0, x4s=x4s, xr4=xr4:
                        e.tensor_tensor(
                    out=x4s[:], in0=pst[:, o0:o0 + 1024], in1=xr4[:],
                    op=ALU.add),
                    reads=pbl + [xr4_b], writes=[x4s_b])
                prog.op("gpsimd", lambda e, tt=tt, oq=oq, x4s=x4s: e.dma_start(
                    out=io["x4"][tt * 128:(tt + 1) * 128,
                                 oq * 1024:(oq + 1) * 1024], in_=x4s[:]),
                    reads=[x4s_b], writes=[io["x4__b"]], dma="x4o")


def build_fused():
    nc = bass.Bass("TRN2", target_bir_lowering=False)
    io = declare_io(nc)
    prog = Prog(nc)
    with ExitStack() as ex:
        c = Ctx(nc, prog, ex)
        c.alloc_psum()
        alloc_globals(c, io)
        R = Region(c, "R", 171)
        emit_l0_attn(c, R, io)
        emit_l0_ffn(c, R, io)
        emit_l1_attn(c, R, io)
        emit_l1_ffn(c, R, io)
        with nc.Block() as block:
            prog.finalize(block)
    return nc


def kernel(L0_attn_q, L0_attn_w_qkv, L0_attn_w_dense, L0_ffn_A_w, L0_ffn_B_w,
           L0_ffn_Wq_w, L0_ffn_keys, L1_attn_wq_a, L1_attn_q_a_ln_w,
           L1_attn_wq_b, L1_attn_wkv_a, L1_attn_kv_a_ln_w, L1_attn_wkv_b,
           L1_attn_wo, L1_ffn_qweight_gu, L1_ffn_qzeros_gu, L1_ffn_scales_gu,
           L1_ffn_qweight_dn, L1_ffn_qzeros_dn, L1_ffn_scales_dn):
    n_cores = 8
    x = np.ascontiguousarray(np.asarray(L0_attn_q, dtype=np.float32))
    wd = host_prep_weights(dict(
        L0_attn_w_qkv=L0_attn_w_qkv, L0_attn_w_dense=L0_attn_w_dense,
        L0_ffn_A_w=L0_ffn_A_w, L0_ffn_B_w=L0_ffn_B_w, L0_ffn_Wq_w=L0_ffn_Wq_w,
        L0_ffn_keys=L0_ffn_keys, L1_attn_wq_a=L1_attn_wq_a,
        L1_attn_q_a_ln_w=L1_attn_q_a_ln_w, L1_attn_wq_b=L1_attn_wq_b,
        L1_attn_wkv_a=L1_attn_wkv_a, L1_attn_kv_a_ln_w=L1_attn_kv_a_ln_w,
        L1_attn_wkv_b=L1_attn_wkv_b, L1_attn_wo=L1_attn_wo,
        L1_ffn_qweight_gu=L1_ffn_qweight_gu, L1_ffn_qzeros_gu=L1_ffn_qzeros_gu,
        L1_ffn_scales_gu=L1_ffn_scales_gu, L1_ffn_qweight_dn=L1_ffn_qweight_dn,
        L1_ffn_qzeros_dn=L1_ffn_qzeros_dn, L1_ffn_scales_dn=L1_ffn_scales_dn))
    nc = build_fused()
    in_maps = [dict(wd, x=x[i]) for i in range(n_cores)]
    res = run_bass_kernel_spmd(nc, in_maps, core_ids=list(range(n_cores)))
    return np.stack([r["x4"] for r in res.results]).astype(np.float32)
